# version 40
# baseline (speedup 1.0000x reference)
import numpy as np
import ml_dtypes
from contextlib import ExitStack

import concourse.bass as bass
import concourse.mybir as mybir
from concourse.bass_utils import run_bass_kernel_spmd

F32 = mybir.dt.float32
BF16 = mybir.dt.bfloat16
I32 = mybir.dt.int32
AF = mybir.ActivationFunctionType
ALU = mybir.AluOpType
AX = mybir.AxisListType.X

S, D, HID = 4096, 1024, 3072
NG, GT = 8, 512
NIC = D // 128
NHC = HID // 128
VHW = D + 1
EPS = 1e-5
ESHIFT = -16.0
QSCALE = 1.0 / 32.0
DEQ = 1.0 / 127.0
DEQ2 = DEQ * DEQ
INV_D = 1.0 / D


class Res:
    def __init__(self):
        self.w = None
        self.r = []

    def rd(self):
        return (self.w,) if self.w else ()

    def wr(self):
        out = list(self.r)
        if self.w:
            out.append(self.w)
        return tuple(out)

    def setw(self, sem, n):
        self.w = (sem, n)
        self.r = []

    def addr(self, sem, n):
        self.r.append((sem, n))


class Plan:
    ENGINES = ("sync", "tensor", "vector", "scalar", "gpsimd")
    SEM_OWNER = {"T": "tensor", "V": "vector", "SC": "scalar"}

    OVL = {
        "WF0": ["WA0", "WA1"],
        "WF1": ["WA1", "PT0", "PT1", "xsq"],
        "WA0": ["WF0"],
        "WA1": ["WF0", "WF1"],
        "PT0": ["WF1"],
        "PT1": ["WF1"],
        "xsq": ["WF1"],
        "HT": ["yT", "QT", "CT", "ztm0", "ztm1", "ztm2", "ztm3"],
        "yT": ["HT", "ztm0", "ztm1", "ztm2", "ztm3"],
        "QT": ["HT"],
        "CT": ["HT"],
        "ytm0": ["zT"], "ytm1": ["zT"], "ytm2": ["zT"], "ytm3": ["zT"],
        "zT": ["ytm0", "ytm1", "ytm2", "ytm3"],
        "ztm0": ["yT", "HT"], "ztm1": ["yT", "HT"],
        "ztm2": ["yT", "HT"], "ztm3": ["yT", "HT"],
        "sg": ["rrow", "bcsb"],
        "rrow": ["sg"], "bcsb": ["sg"],
        "rwg0": ["rwdA", "rwdB", "rwa0", "rwa1"],
        "rwg1": ["rwa1"],
        "rwdA": ["rwg0", "rwa0"], "rwdB": ["rwg0", "rwa0", "rwa1"],
        "rwa0": ["rwg0", "rwdA", "rwdB"],
        "rwa1": ["rwg0", "rwg1", "rwdB"],
    }

    def __init__(self):
        self.streams = {e: [] for e in self.ENGINES}
        self.cnt = {"T": 0, "V": 0, "SC": 0, "din": 0, "dout": 0}
        self.section = {e: [0] for e in self.ENGINES}
        self.R = {}

    def res(self, name):
        if name not in self.R:
            self.R[name] = Res()
        return self.R[name]

    def wW(self, name):
        s = list(self.res(name).wr())
        for o in self.OVL.get(name, ()):
            s += list(self.res(o).wr())
        return tuple(s)

    def wR(self, name):
        return self.res(name).rd()

    def cut(self):
        for e in self.ENGINES:
            self.section[e].append(len(self.streams[e]))

    def emit(self, eng, fn, waits=(), inc=None, inc_by=1):
        w = {}
        for s, v in waits:
            if v is None or v <= 0:
                continue
            assert v <= self.cnt[s], f"wait on future: {s} {v} > {self.cnt[s]} ({eng})"
            if self.SEM_OWNER.get(s) == eng:
                continue
            w[s] = max(w.get(s, 0), v)
        n = None
        if inc is not None:
            self.cnt[inc] += inc_by
            n = self.cnt[inc]
        self.streams[eng].append((tuple(w.items()), fn, inc, inc_by))
        return n


def build_nc(dbg=False, dbg_g=0):
    nc = bass.Bass("TRN2", target_bir_lowering=False, dynamic_dma_scratch_size=2048,
                   detect_race_conditions=False)

    x_h = nc.dram_tensor("x", [S, D], F32, kind="ExternalInput").ap()
    w_h = {}
    for nm in ("wqt", "wkt", "wvt", "wot"):
        w_h[nm] = nc.dram_tensor(nm, [D, D], F32, kind="ExternalInput").ap()
    w_h["wgt"] = nc.dram_tensor("wgt", [D, HID], I32, kind="ExternalInput").ap()
    w_h["wut"] = nc.dram_tensor("wut", [D, HID], I32, kind="ExternalInput").ap()
    w_h["wdt"] = nc.dram_tensor("wdt", [HID, D], I32, kind="ExternalInput").ap()
    ident_h = nc.dram_tensor("ident", [128, 128], BF16, kind="ExternalInput").ap()
    trimask_h = nc.dram_tensor("trimask", [128, 128], BF16, kind="ExternalInput").ap()
    consts_h = nc.dram_tensor("consts", [128, 4], F32, kind="ExternalInput").ap()
    out_h = nc.dram_tensor("out", [S, D], F32, kind="ExternalOutput").ap()

    dbg_h = {}
    if dbg:
        for nm, shp in (("d_stats", [128, 320]), ("d_yT", [128, 4096]),
                        ("d_QT", [128, 4096]), ("d_KT", [128, 4096]),
                        ("d_VH", [128, 1025]), ("d_PT", [128, 512]),
                        ("d_CT", [128, 4096]), ("d_x2", [128, 4096]),
                        ("d_zT", [128, 4096]), ("d_sig", [128, 512]),
                        ("d_HT", [128, 4096])):
            dbg_h[nm] = nc.dram_tensor(nm, shp, F32, kind="ExternalOutput").ap()

    wT_h = {
        "wqt": nc.dram_tensor("wqT", [D, D], BF16, kind="Internal").ap(),
        "wkt": nc.dram_tensor("wkT", [D, D], BF16, kind="Internal").ap(),
        "wvt": nc.dram_tensor("wvT", [D, D], BF16, kind="Internal").ap(),
        "wot": nc.dram_tensor("woT", [D, D], BF16, kind="Internal").ap(),
        "wgt": nc.dram_tensor("wgT", [D, HID], BF16, kind="Internal").ap(),
        "wut": nc.dram_tensor("wuT", [D, HID], BF16, kind="Internal").ap(),
        "wdt": nc.dram_tensor("wdT", [HID, D], BF16, kind="Internal").ap(),
    }

    P = Plan()

    def dma(dst, src):
        return lambda e, d=dst, s=src: e.dma_start(d, s)

    with ExitStack() as ctx:
        ident = ctx.enter_context(nc.sbuf_tensor("identS", [128, 128], BF16))
        trimask = ctx.enter_context(nc.sbuf_tensor("trimaskS", [128, 128], BF16))
        consts = ctx.enter_context(nc.sbuf_tensor("constsS", [128, 4], F32))
        ones_col = ctx.enter_context(nc.sbuf_tensor("ones_col", [128, 128], F32))

        semh = {}
        for s_ in ("din", "dout", "T", "V", "SC"):
            semh[s_] = ctx.enter_context(nc.semaphore(s_))

        KT = ctx.enter_context(nc.sbuf_tensor("KT", [128, NIC * S], BF16))
        VH = ctx.enter_context(nc.sbuf_tensor("VH", [128, 32 * VHW], BF16))
        xtA = ctx.enter_context(nc.sbuf_tensor("xtA", [128, 4096], F32))
        xtB = ctx.enter_context(nc.sbuf_tensor("xtB", [128, 4096], F32))
        xts = [xtA, xtB]
        zT = ctx.enter_context(nc.sbuf_tensor("zT", [128, 4096], BF16))
        R1 = ctx.enter_context(nc.sbuf_tensor("R1", [128, 12288], BF16))
        R2 = ctx.enter_context(nc.sbuf_tensor("R2", [128, 12288], BF16))
        small = ctx.enter_context(nc.sbuf_tensor("small", [128, 512], F32))
        sg = small[:, 0:512]
        stats = ctx.enter_context(nc.sbuf_tensor("stats", [128, 320], F32))

        psAll = ctx.enter_context(nc.psum_tensor("psAll", [128, 3584], F32))
        psT = ctx.enter_context(nc.psum_tensor("psT", [128, 1024], BF16))

        stIN = [KT[:, 16384:22528], KT[:, 22528:28672]]
        stBF = [VH[:, 16400:19472], VH[:, 19472:22544]]
        RWGU = [KT[:, 4096:16384].bitcast(I32), KT[:, 16384:28672].bitcast(I32)]
        RWD = [KT[:, 4096:10240].bitcast(I32), KT[:, 10240:16384].bitcast(I32)]
        spills = {}
        _prolog_par = [0]

        def ensure_weight(nm):
            if nm in spills:
                return
            src = w_h[nm]
            R, C = src.shape
            dt_in = src.dtype
            nb_max = 3072 // C
            r0 = 0
            sp = None
            while r0 < R // 128:
                nb = min(nb_max, R // 128 - r0)
                par = _prolog_par[0] % 2
                _prolog_par[0] += 1
                w_ = nb * C
                din_ap = stIN[par].bitcast(dt_in)
                dd = P.emit("sync",
                            dma(din_ap[:, 0:w_].rearrange("p (b c) -> p b c", b=nb),
                                src[r0 * 128:(r0 + nb) * 128, :]
                                .rearrange("(b p) c -> p b c", p=128)),
                            waits=P.wW(f"stin{par}"), inc="din", inc_by=16)
                P.res(f"stin{par}").setw("din", dd)
                vc = P.emit("vector",
                            lambda e, par_=par, w__=w_, da=din_ap: e.tensor_copy(
                                stBF[par_][:, 0:w__], da[:, 0:w__]),
                            waits=(("din", dd),) + P.wW(f"stbf{par}"), inc="V")
                P.res(f"stbf{par}").setw("V", vc)
                P.res(f"stin{par}").addr("V", vc)
                sp = P.emit("sync",
                            dma(wT_h[nm][r0 * 128:(r0 + nb) * 128, :]
                                .rearrange("(b p) c -> p b c", p=128),
                                stBF[par][:, 0:w_].rearrange("p (b c) -> p b c", b=nb)),
                            waits=(("V", vc),), inc="din", inc_by=16)
                P.res(f"stbf{par}").addr("din", sp)
                r0 += nb
            spills[nm] = sp

        yT = R1[:, 0:4096]
        QT = R1[:, 4096:8192]
        CT = R1[:, 8192:12288]
        HT = R1[:, 0:12288]
        WA = [R2[:, 0:4096], R2[:, 4096:8192]]
        PT = [R2[:, 8192:8704], R2[:, 8704:9216]]
        xsq = R2[:, 10240:11264]
        WF = [R2[:, 0:6144], R2[:, 6144:12288]]

        def psw(c0, c1):
            bks = [f"pb{b}" for b in range(c0 // 512, (c1 - 1) // 512 + 1)]
            w = ()
            for b in bks:
                w += P.wW(b)
            return w, bks

        def psr(c0, c1):
            return [f"pb{b}" for b in range(c0 // 512, (c1 - 1) // 512 + 1)]

        d_ident = P.emit("sync", dma(ident[:], ident_h), inc="din", inc_by=16)
        P.emit("sync", dma(trimask[:], trimask_h), inc="din", inc_by=16)
        d_consts = P.emit("sync", dma(consts[:], consts_h), inc="din", inc_by=16)
        v_ones = P.emit("vector", lambda e: e.memset(ones_col[:], 1.0),
                        waits=(("din", d_consts),), inc="V")

        prev_out_dout = 0
        last_ffn_evac_v = 0

        if dbg:
            dstage = ctx.enter_context(nc.sbuf_tensor("dstage", [128, 832], F32))
            P.res("dstage")

        def dbg_dump(nm, src_ap, waits, res_names=()):
            if not dbg:
                return
            shp = src_ap.shape
            p = shp[0]
            if len(shp) == 3:
                a, b = shp[1], shp[2]
                per = max(1, 832 // b)
                for a0 in range(0, a, per):
                    a1 = min(a0 + per, a)
                    w_ = b * (a1 - a0)
                    vv = P.emit("vector",
                                lambda e, a0_=a0, a1_=a1, w__=w_: e.tensor_copy(
                                    dstage[0:p, 0:w__].rearrange("p (aa bb) -> p aa bb", aa=a1_ - a0_),
                                    src_ap[:, a0_:a1_, :]),
                                waits=tuple(waits) + P.wW("dstage"), inc="V")
                    P.res("dstage").setw("V", vv)
                    for rn in res_names:
                        P.res(rn).addr("V", vv)
                    dd = P.emit("sync", dma(dbg_h[nm][0:p, a0 * b:a1 * b],
                                            dstage[0:p, 0:w_]),
                                waits=(("V", vv),), inc="dout", inc_by=16)
                    P.res("dstage").addr("dout", dd)
                return
            flat = shp[1]
            for c0 in range(0, flat, 832):
                c1 = min(c0 + 832, flat)
                vv = P.emit("vector",
                            lambda e, c0_=c0, c1_=c1: e.tensor_copy(
                                dstage[0:p, 0:c1_ - c0_], src_ap[:, c0_:c1_]),
                            waits=tuple(waits) + P.wW("dstage"), inc="V")
                P.res("dstage").setw("V", vv)
                for rn in res_names:
                    P.res(rn).addr("V", vv)
                dd = P.emit("sync", dma(dbg_h[nm][0:p, c0:c1], dstage[0:p, 0:c1 - c0]),
                            waits=(("V", vv),), inc="dout", inc_by=16)
                P.res("dstage").addr("dout", dd)

        def ln_stats_chunk(stats_base, xt, xres, t):
            st = stats[:, stats_base + t * 16: stats_base + t * 16 + 16]
            sres = f"st{stats_base}_{t}"
            P.res(sres)
            v = P.emit("vector",
                       lambda e, t_=t, st_=st, xt_=xt: e.scalar_tensor_tensor(
                           xsq[:, 0:1024], xt_[:, t_ * 1024:(t_ + 1) * 1024], 1.0,
                           xt_[:, t_ * 1024:(t_ + 1) * 1024], ALU.mult, ALU.mult,
                           accum_out=st_[:, 1:2]),
                       waits=P.wR(xres) + P.wW("xsq") + P.wW(sres), inc="V")
            P.res("xsq").setw("V", v)
            P.emit("vector",
                   lambda e, t_=t, st_=st, xt_=xt: e.reduce_sum(
                       st_[:, 0:1], xt_[:, t_ * 1024:(t_ + 1) * 1024], axis=AX))

        def ln_chain_apply(tm_prefix, tm_region, stats_base, xt):
            v_var = None
            for t in range(4):
                st = stats[:, stats_base + t * 16: stats_base + t * 16 + 16]
                P.emit("vector", lambda e, st_=st: e.tensor_scalar_mul(st_[:, 2:3], st_[:, 0:1], INV_D))
                P.emit("vector", lambda e, st_=st: e.tensor_scalar_mul(st_[:, 3:4], st_[:, 1:2], INV_D))
                v_var = P.emit("vector", lambda e, st_=st: e.tensor_scalar(
                    st_[:, 4:5], st_[:, 2:3], st_[:, 2:3], st_[:, 3:4],
                    ALU.mult, ALU.subtract), inc="V")
            s_sq = None
            for t in range(4):
                st = stats[:, stats_base + t * 16: stats_base + t * 16 + 16]
                s_sq = P.emit("scalar",
                              lambda e, st_=st: e.activation(st_[:, 6:7], st_[:, 4:5], AF.Sqrt,
                                                             scale=-1.0, bias=consts[:, 1:2]),
                              waits=(("V", v_var), ("din", d_consts)), inc="SC")
            for t in range(4):
                st = stats[:, stats_base + t * 16: stats_base + t * 16 + 16]
                P.emit("vector", lambda e, st_=st: e.reciprocal(st_[:, 7:8], st_[:, 6:7]),
                       waits=(("SC", s_sq),))
            v_rs = []
            for t in range(4):
                st = stats[:, stats_base + t * 16: stats_base + t * 16 + 16]
                v_rs.append(P.emit("vector",
                            lambda e, st_=st: e.scalar_tensor_tensor(
                                st_[:, 8:9], st_[:, 2:3], -1.0, st_[:, 7:8],
                                ALU.mult, ALU.mult),
                            inc="V"))
            sa_list = []
            for t in range(4):
                st = stats[:, stats_base + t * 16: stats_base + t * 16 + 16]
                tmr = f"{tm_prefix}{t}"
                tm = tm_region[:, t * 1024:(t + 1) * 1024]
                sa = P.emit("scalar",
                            lambda e, t_=t, st_=st, tm_=tm, xt_=xt: e.activation(
                                tm_, xt_[:, t_ * 1024:(t_ + 1) * 1024], AF.Identity,
                                scale=st_[:, 7:8], bias=st_[:, 8:9]),
                            waits=(("V", v_rs[t]),) + P.wW(tmr), inc="SC")
                P.res(tmr).setw("SC", sa)
                P.res(f"st{stats_base}_{t}").addr("SC", sa)
                sa_list.append(sa)
            return sa_list

        def emit_ln_pre(tm_prefix, tm_region, stats_base, xt, xres):
            for t in range(4):
                ln_stats_chunk(stats_base, xt, xres, t)
            return ln_chain_apply(tm_prefix, tm_region, stats_base, xt)

        def emit_ln_post(tm_prefix, tm_region, dst_rearr, dst_name, sa_list):
            for t in range(4):
                tmr = f"{tm_prefix}{t}"
                tm = tm_region[:, t * 1024:(t + 1) * 1024]
                sa = sa_list[t]
                tlast = None
                for ic in range(NIC):
                    tlast = P.emit("tensor",
                                   lambda e, ic_=ic, tm_=tm: e.transpose(
                                       psT[:, ic_ * 128:(ic_ + 1) * 128],
                                       tm_[:, ic_ * 128:(ic_ + 1) * 128], ident[:]),
                                   waits=(("SC", sa), ("din", d_ident)) +
                                         (P.wW("psT") if ic == 0 else ()),
                                   inc="T" if ic == NIC - 1 else None)
                P.res("psT").setw("T", tlast)
                P.res(tmr).addr("T", tlast)
                ve = P.emit("vector",
                            lambda e, t_=t: e.tensor_copy(
                                dst_rearr[:, :, t_ * 128:(t_ + 1) * 128],
                                psT[:].rearrange("p (k c) -> p k c", k=NIC)),
                            waits=(("T", tlast),) + (P.wW(dst_name) if t == 0 else ()),
                            inc="V")
                P.res("psT").addr("V", ve)
                P.res(dst_name).setw("V", ve)

        pool_cols = [(0, 512), (512, 1024), (1024, 1536), (1536, 2048)]

        dout_of_parity = [0, 0]

        def x_load(gg):
            par = gg % 2
            dx = P.emit("sync",
                        dma(xts[par][:].rearrange("p (t c) -> p t c", t=4),
                            x_h[gg * GT:(gg + 1) * GT, :].rearrange("(t p) c -> p t c", p=128)),
                        waits=(("dout", dout_of_parity[par]),) + P.wW(f"xt{par}"),
                        inc="din", inc_by=16)
            P.res(f"xt{par}").setw("din", dx)

        def wa_load2(nm, half, gg, extra=()):
            if gg == 0:
                ensure_weight(nm)
            dd = P.emit("sync",
                        dma(WA[half].rearrange("p (k c) -> p k c", k=NIC),
                            wT_h[nm].rearrange("(k p) o -> p k o", p=128)
                            [:, :, half * 512:(half + 1) * 512]),
                        waits=P.wW(f"WA{half}") + tuple(extra) +
                              ((("din", spills[nm]),) if gg == 0 else ()),
                        inc="din", inc_by=16)
            P.res(f"WA{half}").setw("din", dd)
            return dd

        RWA = [KT[:, 4096:12288].bitcast(F32), KT[:, 12288:20480].bitcast(F32)]
        _wa_par = [0]

        def wa_direct(nm, half, extra=()):
            par = _wa_par[0] % 2
            _wa_par[0] += 1
            rr = f"rwa{par}"
            dd = P.emit("sync",
                        dma(RWA[par].rearrange("p (k c) -> p k c", k=NIC),
                            w_h[nm].rearrange("(k p) o -> p k o", p=128)
                            [:, :, half * 512:(half + 1) * 512]),
                        waits=P.wW(rr), inc="din", inc_by=16)
            P.res(rr).setw("din", dd)
            vc = P.emit("vector",
                        lambda e, h_=half, par_=par: e.tensor_copy(
                            WA[h_][:, 0:4096], RWA[par_][:, 0:4096]),
                        waits=(("din", dd),) + P.wW(f"WA{half}") + tuple(extra),
                        inc="V")
            P.res(f"WA{half}").setw("V", vc)
            P.res(rr).addr("V", vc)
            sp = P.emit("sync",
                        dma(wT_h[nm].rearrange("(k p) o -> p k o", p=128)
                            [:, :, half * 512:(half + 1) * 512],
                            WA[half].rearrange("p (k c) -> p k c", k=NIC)),
                        waits=(("V", vc),), inc="din", inc_by=16)
            P.res(f"WA{half}").addr("din", sp)
            spills[nm] = sp
            return ("V", vc)

        def wa_get(nm, half, gg, extra=()):
            if gg == 0:
                return wa_direct(nm, half, extra)
            return ("din", wa_load2(nm, half, gg, extra))

        def stage_wq(ng):
            nonlocal wq_d_next
            kw = P.wW(f"wqsK{ng}")
            if ng in (2, 3):
                kw = kw + P.wW("rwg0")
            if ng in (4, 5, 6):
                kw = kw + P.wW("rwg1") + P.wW("stin0") + P.wW("stin1")
            dK = P.emit("sync",
                        dma(KT[:, 4096 * ng: 4096 * ng + 4096]
                            .rearrange("p (k c) -> p k c", k=NIC),
                            wT_h["wqt"].rearrange("(k p) o -> p k o", p=128)
                            [:, :, 0:512]),
                        waits=kw, inc="din", inc_by=16)
            P.res(f"wqsK{ng}").setw("din", dK)
            vw = P.wW(f"wqsV{ng}")
            if ng in (2, 3):
                vw = vw + P.wW("gu2") + P.wW("gu3")
            if ng in (4, 5):
                vw = vw + P.wW("stbf0") + P.wW("stbf1")
            dV = P.emit("sync",
                        dma(VH[:, 4100 * ng: 4100 * ng + 4096]
                            .rearrange("p (k c) -> p k c", k=NIC),
                            wT_h["wqt"].rearrange("(k p) o -> p k o", p=128)
                            [:, :, 512:1024]),
                        waits=vw, inc="din", inc_by=16)
            P.res(f"wqsV{ng}").setw("din", dV)
            wq_d_next = [("din", dK), ("din", dV)]

        def emit_store(ps_):
            gg, xt_, xres_, evac_v = ps_
            so = P.emit("sync",
                        dma(out_h[gg * GT:(gg + 1) * GT, :].rearrange("(t p) c -> p t c", p=128),
                            xt_[:].rearrange("p (t c) -> p t c", t=4)),
                        waits=(("V", evac_v),),
                        inc="dout", inc_by=16)
            P.res(xres_).addr("dout", so)
            dout_of_parity[gg % 2] = P.cnt["dout"]

        pending_store = None
        x_load(0)
        wq_d_next = [wa_get("wqt", 0, 0), wa_get("wqt", 1, 0)]
        ln1_sa_next = emit_ln_pre("ytm", zT, 0, xts[0], "xt0")

        for g in range(NG):
            xt = xts[g % 2]
            xres = f"xt{g % 2}"
            wq_d = wq_d_next
            ln1_sa = ln1_sa_next

            yT_r = yT.rearrange("p (k c) -> p k c", k=NIC)
            emit_ln_post("ytm", zT, yT_r, "yT", ln1_sa)
            if dbg and g == dbg_g:
                allw = (("V", P.cnt["V"]), ("SC", P.cnt["SC"]))
                dbg_dump("d_stats", stats[:], allw)
                dbg_dump("d_yT", yT, allw, ("yT",))

            pool_i = 0

            def qk_proj(w_d, evac, bufs=None, bufres=None):
                nonlocal pool_i
                if bufs is None:
                    bufs = (WA[0], WA[1])
                    bufres = ("WA0", "WA1")
                mm_done = [0, 0]
                for half in range(2):
                    for pl in range(4):
                        pc = half * 4 + pl
                        c0, c1 = pool_cols[pool_i % 4]; pool_i += 1
                        ps = psAll[:, c0:c1]
                        w0, bks = psw(c0, c1)
                        tlast = None
                        for ic in range(NIC):
                            tlast = P.emit(
                                "tensor",
                                lambda e, b_=bufs[half], pl_=pl, ic_=ic, ps_=ps: e.matmul(
                                    ps_,
                                    b_[:, ic_ * 512 + pl_ * 128: ic_ * 512 + (pl_ + 1) * 128],
                                    yT[:, ic_ * 512:(ic_ + 1) * 512],
                                    start=(ic_ == 0), stop=(ic_ == NIC - 1)),
                                waits=((w_d[half],) + P.wR("yT") + w0)
                                      if ic == 0 else (),
                                inc="T" if ic == NIC - 1 else None)
                        for b in bks:
                            P.res(b).setw("T", tlast)
                        P.res("yT").addr("T", tlast)
                        P.res(bufres[half]).addr("T", tlast)
                        sem, n = evac(pc, ps, tlast)
                        for b in bks:
                            P.res(b).addr(sem, n)
                    mm_done[half] = P.cnt["T"]
                return mm_done

            def q_evac(pc, ps, tlast):
                n = P.emit("scalar",
                           lambda e, pc_=pc, ps_=ps: e.activation(
                               QT[:, pc_ * 512:(pc_ + 1) * 512], ps_, AF.Copy, scale=QSCALE),
                           waits=(("T", tlast),) + (P.wW("QT") if pc == 0 else ()),
                           inc="SC")
                P.res("QT").setw("SC", n)
                return ("SC", n)

            if g >= 2:
                q_done = qk_proj(wq_d, q_evac,
                                 bufs=(KT[:, 4096 * g: 4096 * g + 4096],
                                       VH[:, 4100 * g: 4100 * g + 4096]),
                                 bufres=(f"wqsK{g}", f"wqsV{g}"))
                wk_d = [wa_get("wkt", 0, g), wa_get("wkt", 1, g)]
            else:
                q_done = qk_proj(wq_d, q_evac)
                wk_d = [wa_get("wkt", 0, g, (("T", q_done[0]),)),
                        wa_get("wkt", 1, g, (("T", q_done[1]),))]

            def k_evac(pc, ps, tlast):
                extra = ()
                if pc == 0:
                    extra = P.wW("KT")
                    if g in (1, 2, 3):
                        extra = extra + P.wW("rwg0")
                    if g in (4, 5, 6):
                        extra = extra + P.wW("stin0") + P.wW("stin1") + P.wW("rwg1")
                    if g >= 2:
                        extra = extra + P.wW(f"wqsK{g}")
                n = P.emit("vector",
                           lambda e, pc_=pc, ps_=ps, gg=g: e.tensor_copy(
                               KT[:, gg * 4096: (gg + 1) * 4096]
                               .rearrange("p (jj c) -> p jj c", jj=4)
                               [:, :, pc_ * 128:(pc_ + 1) * 128],
                               ps_.rearrange("p (jj c) -> p jj c", jj=4)),
                           waits=(("T", tlast),) + extra,
                           inc="V")
                P.res("KT").setw("V", n)
                return ("V", n)

            if pending_store is not None:
                emit_store(pending_store)
                pending_store = None
            k_done = qk_proj(wk_d, k_evac)
            wv_d = [wa_get("wvt", 0, g, (("T", k_done[0]),)),
                    wa_get("wvt", 1, g, (("T", k_done[1]),))]

            v_done = [0, 0]
            for dh in range(2):
                for ssub in range(4):
                    sidx = 4 * g + ssub
                    c0, c1 = pool_cols[pool_i % 4]; pool_i += 1
                    ps = psAll[:, c0:c1]
                    w0, bks = psw(c0, c1)
                    tlast = None
                    for ic in range(NIC):
                        tlast = P.emit(
                            "tensor",
                            lambda e, dh_=dh, ss_=ssub, ic_=ic, ps_=ps: e.matmul(
                                ps_,
                                yT[:, ic_ * 512 + ss_ * 128: ic_ * 512 + (ss_ + 1) * 128],
                                WA[dh_][:, ic_ * 512:(ic_ + 1) * 512],
                                start=(ic_ == 0), stop=(ic_ == NIC - 1)),
                            waits=((wv_d[dh],) + P.wR("yT") + w0)
                                  if ic == 0 else (),
                            inc="T" if ic == NIC - 1 else None)
                    for b in bks:
                        P.res(b).setw("T", tlast)
                    P.res("yT").addr("T", tlast)
                    P.res(f"WA{dh}").addr("T", tlast)
                    ve = P.emit("vector",
                                lambda e, si=sidx, dh_=dh, ps_=ps: e.tensor_copy(
                                    VH[:, si * VHW + dh_ * 512: si * VHW + (dh_ + 1) * 512],
                                    ps_),
                                waits=(("T", tlast),) +
                                      ((P.wW("VH") +
                                        (P.wW("stbf0") + P.wW("stbf1")
                                         if g in (4, 5) else ()) +
                                        (P.wW("gu2") if g in (1, 2) else ()) +
                                        (P.wW("gu3") if g in (2, 3) else ()) +
                                        (P.wW(f"wqsV{g}") if g >= 2 else ()))
                                       if (dh == 0 and ssub == 0) else ()),
                                inc="V")
                    for b in bks:
                        P.res(b).addr("V", ve)
                    P.res("VH").setw("V", ve)
                v_done[dh] = P.cnt["T"]
            vo = None
            for ssub in range(4):
                sidx = 4 * g + ssub
                vo = P.emit("vector",
                            lambda e, si=sidx: e.memset(
                                VH[:, si * VHW + 1024: si * VHW + 1025], 1.0),
                            inc="V")
            P.res("VH").setw("V", vo)

            wo_d = [wa_get("wot", 0, g, (("T", v_done[0]),)),
                    wa_get("wot", 1, g, (("T", v_done[1]),))]

            if dbg and g == dbg_g:
                allw = (("V", P.cnt["V"]), ("SC", P.cnt["SC"]))
                dbg_dump("d_QT", QT, allw, ("QT",))
                dbg_dump("d_KT", KT[:].rearrange("p (k c) -> p k c", k=NIC)[:, :, 0:512],
                         allw, ("KT",))
                dbg_dump("d_VH", VH[:, 0:VHW], allw, ("VH",))

            qs_ready = P.cnt["SC"]
            kv_ready = P.cnt["V"]
            ct_marks = []
            for qt in range(2):
                NSB = 4 * g + 2 * qt + 2
                NB = NSB // 2
                pv_last = None
                se_of = {}
                fix_of = {}

                def emit_scores(i, qt=qt, NB=NB):
                    par = i % 2
                    sc0 = 512 * par
                    w0, sbks = psw(sc0, sc0 + 512)
                    if qt == 0 and i == 0:
                        w0 = w0 + (("SC", qs_ready), ("V", kv_ready))
                    tlast = None
                    first = True
                    for jj in range(2):
                        j = 2 * i + jj
                        for ic in range(NIC):
                            tlast = P.emit(
                                "tensor",
                                lambda e, j_=j, jj_=jj, ic_=ic, sc0_=sc0, qt_=qt: e.matmul(
                                    psAll[:, sc0_ + jj_ * 256: sc0_ + (jj_ + 1) * 256],
                                    KT[:, j_ * 1024 + ic_ * 128: j_ * 1024 + (ic_ + 1) * 128],
                                    QT[:, ic_ * 512 + qt_ * 256: ic_ * 512 + qt_ * 256 + 256],
                                    start=(ic_ == 0), stop=(ic_ == NIC - 1)),
                                waits=tuple(w0) if first else (),
                                inc="T" if (jj == 1 and ic == NIC - 1) else None)
                            first = False
                    for b in sbks:
                        P.res(b).setw("T", tlast)
                    se = P.emit("scalar",
                                lambda e, sc0_=sc0, par_=par: e.activation(
                                    PT[par_], psAll[:, sc0_:sc0_ + 512], AF.Exp,
                                    bias=consts[:, 0:1]),
                                waits=(("T", tlast),) + P.wW(f"PT{par}"), inc="SC")
                    for b in sbks:
                        P.res(b).addr("SC", se)
                    P.res(f"PT{par}").setw("SC", se)
                    se_of[i] = se
                    vfix = None
                    for jj in range(2):
                        j = 2 * i + jj
                        b_ = j - (4 * g + 2 * qt)
                        if b_ < 0:
                            continue
                        if b_ > 0:
                            P.emit("vector",
                                   lambda e, par_=par, jj_=jj, b__=b_: e.memset(
                                       PT[par_][:, jj_ * 256: jj_ * 256 + b__ * 128], 0.0),
                                   waits=(("SC", se),))
                        vfix = P.emit("vector",
                                      lambda e, par_=par, jj_=jj, b__=b_: e.tensor_tensor(
                                          PT[par_][:, jj_ * 256 + b__ * 128: jj_ * 256 + (b__ + 1) * 128],
                                          PT[par_][:, jj_ * 256 + b__ * 128: jj_ * 256 + (b__ + 1) * 128],
                                          trimask[:], ALU.mult),
                                      waits=(("SC", se), ("din", d_consts)), inc="V")
                        P.res(f"PT{par}").addr("V", vfix)
                    if vfix is not None:
                        fix_of[i] = vfix
                    if dbg and g == dbg_g and qt == 0 and i == 0:
                        dbg_dump("d_PT", PT[0][:, 0:512],
                                 (("SC", se),) + ((("V", vfix),) if vfix else ()),
                                 ("PT0",))

                emit_scores(0)
                for i in range(NB):
                    if i + 1 < NB:
                        emit_scores(i + 1)
                    par = i % 2
                    se = se_of[i]
                    vfix = fix_of.get(i)
                    wpv = [("SC", se)]
                    if vfix is not None:
                        wpv.append(("V", vfix))
                    if i == 0:
                        wc, cbks = psw(1024, 3328)
                        wpv += list(wc) + list(P.wR("VH"))
                    else:
                        cbks = psr(1024, 3328)
                    tpv = None
                    firstpv = True
                    for jj in range(2):
                        j = 2 * i + jj
                        for dc in range(NIC):
                            tpv = P.emit(
                                "tensor",
                                lambda e, j_=j, jj_=jj, dc_=dc, par_=par, i_=i, NB_=NB: e.matmul(
                                    psAll[:, 1024 + dc_ * 256: 1024 + (dc_ + 1) * 256],
                                    VH[:, j_ * VHW + dc_ * 128: j_ * VHW + (dc_ + 1) * 128],
                                    PT[par_][:, jj_ * 256:(jj_ + 1) * 256],
                                    start=(i_ == 0 and jj_ == 0 and dc_ % 2 == 0),
                                    stop=(i_ == NB_ - 1 and jj_ == 1),
                                    skip_group_check=True),
                                waits=tuple(wpv) if firstpv else (),
                                inc=None)
                            firstpv = False
                        tpv = P.emit(
                            "tensor",
                            lambda e, j_=j, jj_=jj, par_=par, i_=i, NB_=NB: e.matmul(
                                psAll[0:1, 3072:3328],
                                VH[:, j_ * VHW + 1024: j_ * VHW + 1025],
                                PT[par_][:, jj_ * 256:(jj_ + 1) * 256],
                                start=(i_ == 0 and jj_ == 0),
                                stop=(i_ == NB_ - 1 and jj_ == 1),
                                skip_group_check=True),
                            inc="T" if jj == 1 else None)
                    for b in cbks:
                        P.res(b).setw("T", tpv)
                    P.res(f"PT{par}").addr("T", tpv)
                    pv_last = tpv
                vr = P.emit("vector",
                            lambda e: e.reciprocal(small[0:1, 0:256], psAll[0:1, 3072:3328]),
                            waits=(("T", pv_last),) + P.wW("rrow"), inc="V")
                P.res("rrow").setw("V", vr)
                for b in psr(3072, 3328):
                    P.res(b).addr("V", vr)
                tb = P.emit("tensor",
                            lambda e: e.matmul(psAll[:, 3328:3584], ones_col[0:1, :],
                                               small[0:1, 0:256], start=True, stop=True),
                            waits=(("V", vr), ("V", v_ones)), inc="T")
                P.res("pb6").setw("T", tb)
                vb = P.emit("vector",
                            lambda e: e.tensor_copy(small[:, 256:512], psAll[:, 3328:3584]),
                            waits=(("T", tb),) + P.wW("bcsb"), inc="V")
                P.res("pb6").addr("V", vb)
                P.res("bcsb").setw("V", vb)
                if dbg and g == dbg_g and qt == 0:
                    dbg_dump("d_sig", small[:, 0:512], (("V", vb),), ("bcsb", "rrow"))
                vm = None
                for dc in range(NIC):
                    vm = P.emit("vector",
                                lambda e, dc_=dc, qt_=qt: e.tensor_tensor(
                                    CT[:, dc_ * 512 + qt_ * 256: dc_ * 512 + qt_ * 256 + 256],
                                    psAll[:, 1024 + dc_ * 256: 1024 + (dc_ + 1) * 256],
                                    small[:, 256:512], ALU.mult),
                                waits=(("V", vb),) +
                                      (P.wW("CT") if (qt == 0 and dc == 0) else ()),
                                inc="V")
                    for b in psr(1024 + dc * 256, 1024 + (dc + 1) * 256):
                        P.res(b).addr("V", vm)
                P.res("bcsb").addr("V", vm)
                P.res("CT").setw("V", vm)
                ct_marks.append(("V", vm))

            P.res("QT").addr("T", P.cnt["T"])
            if dbg and g == dbg_g:
                dbg_dump("d_CT", CT, tuple(ct_marks), ("CT",))

            for ssub in range(4):
                for dh in range(2):
                    c0, c1 = pool_cols[pool_i % 4]; pool_i += 1
                    ps = psAll[:, c0:c1]
                    w0, bks = psw(c0, c1)
                    tlast = None
                    for ic in range(NIC):
                        tlast = P.emit(
                            "tensor",
                            lambda e, dh_=dh, ss_=ssub, ic_=ic, ps_=ps: e.matmul(
                                ps_,
                                CT[:, ic_ * 512 + ss_ * 128: ic_ * 512 + (ss_ + 1) * 128],
                                WA[dh_][:, ic_ * 512:(ic_ + 1) * 512],
                                start=(ic_ == 0), stop=(ic_ == NIC - 1)),
                            waits=(tuple(ct_marks) + (wo_d[dh],) + w0)
                                  if ic == 0 else (),
                            inc="T" if ic == NIC - 1 else None)
                    for b in bks:
                        P.res(b).setw("T", tlast)
                    P.res("CT").addr("T", tlast)
                    P.res(f"WA{dh}").addr("T", tlast)
                    ve = P.emit("vector",
                                lambda e, dh_=dh, ss_=ssub, ps_=ps, xt_=xt: e.scalar_tensor_tensor(
                                    xt_[:, ss_ * 1024 + dh_ * 512: ss_ * 1024 + (dh_ + 1) * 512],
                                    ps_, 1.0,
                                    xt_[:, ss_ * 1024 + dh_ * 512: ss_ * 1024 + (dh_ + 1) * 512],
                                    ALU.mult, ALU.add,
                                    accum_out=stats[:, 160 + ss_ * 16 + 9 + dh_:
                                                    160 + ss_ * 16 + 10 + dh_]),
                                waits=(("T", tlast),), inc="V")
                    for b in bks:
                        P.res(b).addr("V", ve)
                    P.res(xres).setw("V", ve)
                st_ = stats[:, 160 + ssub * 16: 160 + ssub * 16 + 16]
                P.res(f"st160_{ssub}")
                vq = P.emit("vector",
                            lambda e, t_=ssub, stx=st_, xt_=xt: e.scalar_tensor_tensor(
                                xsq[:, 0:1024], xt_[:, t_ * 1024:(t_ + 1) * 1024], 1.0,
                                xt_[:, t_ * 1024:(t_ + 1) * 1024], ALU.mult, ALU.mult,
                                accum_out=stx[:, 1:2]),
                            waits=P.wR(xres) + P.wW("xsq") + P.wW(f"st160_{ssub}"),
                            inc="V")
                P.res("xsq").setw("V", vq)
                P.emit("vector",
                       lambda e, stx=st_: e.tensor_scalar(
                           stx[:, 0:1], stx[:, 9:10], stx[:, 10:11], 1.0,
                           ALU.add, ALU.mult))

            if dbg and g == dbg_g:
                dbg_dump("d_x2", xt[:], (("V", P.cnt["V"]),), (xres,))

            zT_r = zT[:].rearrange("p (k c) -> p k c", k=NIC)
            ln2_sa = ln_chain_apply("ztm", HT, 160, xt)
            emit_ln_post("ztm", HT, zT_r, "zT", ln2_sa)
            if dbg and g == dbg_g:
                dbg_dump("d_zT", zT[:], (("V", P.cnt["V"]), ("SC", P.cnt["SC"])), ("zT",))

            gu_d = []

            if g == 0:
                SLW, HPS, NSL = 768, 6, 4
                gu_bufs = [(WF[0], "WF0", WF[1], "WF1"),
                           (VH[:, 4100:10244], "gu2", VH[:, 10244:16388], "gu3")]

                gu_pending = {}

                def gu_flush_spill(br):
                    nm, buf, sl_, vc = gu_pending.pop(br)
                    sp = P.emit("sync",
                                dma(wT_h[nm].rearrange("(k p) hh -> p k hh", p=128)
                                    [:, :, sl_ * SLW:(sl_ + 1) * SLW],
                                    buf[:, 0:6144].rearrange("p (k c) -> p k c", k=NIC)),
                                waits=(("SC", vc),), inc="din", inc_by=16)
                    P.res(br).addr("din", sp)
                    spills[nm] = sp

                def gu_slab_load(sl):
                    gb, gr, ub, ur = gu_bufs[sl % 2]
                    pair = []
                    for nm, buf, br, rw in (("wgt", gb, gr, 0), ("wut", ub, ur, 1)):
                        dd = P.emit("sync",
                                    dma(RWGU[rw].rearrange("p (k c) -> p k c", k=NIC),
                                        w_h[nm].rearrange("(k p) hh -> p k hh", p=128)
                                        [:, :, sl * SLW:(sl + 1) * SLW]),
                                    waits=P.wW(f"rwg{rw}") +
                                          ((P.wW("stin0") + P.wW("stin1"))
                                           if rw == 1 else ()),
                                    inc="din", inc_by=16)
                        P.res(f"rwg{rw}").setw("din", dd)
                        if br in gu_pending:
                            gu_flush_spill(br)
                        vc = P.emit("scalar",
                                    lambda e, b_=buf, rw_=rw: e.activation(
                                        b_[:, 0:6144], RWGU[rw_][:, 0:6144], AF.Copy),
                                    waits=(("din", dd),) + P.wW(br), inc="SC")
                        P.res(br).setw("SC", vc)
                        P.res(f"rwg{rw}").addr("SC", vc)
                        gu_pending[br] = (nm, buf, sl, vc)
                        pair.append(("SC", vc))
                    gu_d.append(tuple(pair))

                def gu_lhs(sl, half, ic, hcl):
                    gb, gr, ub, ur = gu_bufs[sl % 2]
                    buf = gb if half == 0 else ub
                    return buf[:, ic * SLW + hcl * 128: ic * SLW + (hcl + 1) * 128]

                def gu_mark_readers(sl, tu):
                    gb, gr, ub, ur = gu_bufs[sl % 2]
                    P.res(gr).addr("T", tu)
                    P.res(ur).addr("T", tu)
            else:
                SLW, HPS, NSL = 384, 3, 8

                def gu_slab_load(sl):
                    p_ = sl % 2
                    wgd = P.emit("sync",
                                 dma(WF[p_][:, 0:3072].rearrange("p (k c) -> p k c", k=NIC),
                                     wT_h["wgt"].rearrange("(k p) hh -> p k hh", p=128)
                                     [:, :, sl * 384:(sl + 1) * 384]),
                                 waits=P.wW(f"WF{p_}"),
                                 inc="din", inc_by=16)
                    wud = P.emit("sync",
                                 dma(WF[p_][:, 3072:6144].rearrange("p (k c) -> p k c", k=NIC),
                                     wT_h["wut"].rearrange("(k p) hh -> p k hh", p=128)
                                     [:, :, sl * 384:(sl + 1) * 384]),
                                 inc="din", inc_by=16)
                    P.res(f"WF{p_}").setw("din", wud)
                    gu_d.append((("din", wgd), ("din", wud)))

                def gu_lhs(sl, half, ic, hcl):
                    p_ = sl % 2
                    base = half * 3072 + ic * SLW + hcl * 128
                    return WF[p_][:, base: base + 128]

                def gu_mark_readers(sl, tu):
                    P.res(f"WF{sl % 2}").addr("T", tu)

            gu_slab_load(0)
            gu_slab_load(1)

            if 1 <= g < NG - 1:
                x_load(g + 1)
                stage_wq(g + 1)

            GUC = [(2048, 2560), (2560, 3072)]
            h_done_v = 0
            for hc in range(NHC):
                sl, hcl = hc // HPS, hc % HPS
                wgw, wuw = gu_d[sl]
                gw, gbks = psw(*GUC[0])
                tg = None
                for ic in range(NIC):
                    tg = P.emit("tensor",
                                lambda e, ic_=ic, a_=gu_lhs(sl, 0, ic, hcl): e.matmul(
                                    psAll[:, GUC[0][0]:GUC[0][1]],
                                    a_,
                                    zT[:, ic_ * 512:(ic_ + 1) * 512],
                                    start=(ic_ == 0), stop=(ic_ == NIC - 1)),
                                waits=((wgw,) + P.wR("zT") + gw)
                                      if ic == 0 else (),
                                inc="T" if ic == NIC - 1 else None)
                for b in gbks:
                    P.res(b).setw("T", tg)
                P.res("zT").addr("T", tg)
                uw, ubks = psw(*GUC[1])
                tu = None
                for ic in range(NIC):
                    tu = P.emit("tensor",
                                lambda e, ic_=ic, a_=gu_lhs(sl, 1, ic, hcl): e.matmul(
                                    psAll[:, GUC[1][0]:GUC[1][1]],
                                    a_,
                                    zT[:, ic_ * 512:(ic_ + 1) * 512],
                                    start=(ic_ == 0), stop=(ic_ == NIC - 1)),
                                waits=((wuw,) + uw) if ic == 0 else (),
                                inc="T" if ic == NIC - 1 else None)
                for b in ubks:
                    P.res(b).setw("T", tu)
                if hcl == HPS - 1:
                    gu_mark_readers(sl, tu)
                    if sl + 2 < NSL:
                        gu_slab_load(sl + 2)
                ssil = P.emit("scalar",
                              lambda e: e.activation(sg[:], psAll[:, GUC[0][0]:GUC[0][1]],
                                                     AF.Silu, scale=DEQ),
                              waits=(("T", tg),) + P.wW("sg"), inc="SC")
                for b in gbks:
                    P.res(b).addr("SC", ssil)
                P.res("sg").setw("SC", ssil)
                vh_ = P.emit("vector",
                             lambda e, hc_=hc: e.tensor_tensor(
                                 HT[:, hc_ * 512:(hc_ + 1) * 512], sg[:],
                                 psAll[:, GUC[1][0]:GUC[1][1]], ALU.mult),
                             waits=(("SC", ssil), ("T", tu)) +
                                   (P.wW("HT") if hc == 0 else ()),
                             inc="V")
                for b in ubks:
                    P.res(b).addr("V", vh_)
                P.res("sg").addr("V", vh_)
                P.res("HT").setw("V", vh_)
                h_done_v = vh_

            if g == 0:
                for br_ in list(gu_pending):
                    gu_flush_spill(br_)

            if dbg and g == dbg_g:
                dbg_dump("d_HT", HT[:, 0:4096], (("V", P.cnt["V"]),), ("HT",))

            if g == 0 and g + 1 < NG:
                x_load(g + 1)
            if g + 1 < NG:
                ln1_sa_next = emit_ln_pre("ytm", zT, 0, xts[(g + 1) % 2],
                                          f"xt{(g + 1) % 2}")

            for dh in range(2):
                wd_d = []

                if g == 0:
                    def wd_slab_load(sl, dh=None):
                        par = sl % 2
                        rr = "rwdA" if par == 0 else "rwdB"
                        dd = P.emit("sync",
                                    dma(RWD[par].rearrange("p (k c) -> p k c", k=6),
                                        w_h["wdt"].rearrange("(k p) dd -> p k dd", p=128)
                                        [:, 6 * sl:6 * sl + 6, dh * 512:(dh + 1) * 512]),
                                    waits=P.wW(rr), inc="din", inc_by=16)
                        P.res(rr).setw("din", dd)
                        vc = P.emit("scalar",
                                    lambda e, par_=par: e.activation(
                                        WF[par_][:, 0:3072], RWD[par_][:, 0:3072], AF.Copy),
                                    waits=(("din", dd),) + P.wW(f"WF{par}"), inc="SC")
                        P.res(f"WF{par}").setw("SC", vc)
                        P.res(rr).addr("SC", vc)
                        sp = P.emit("sync",
                                    dma(wT_h["wdt"].rearrange("(k p) dd -> p k dd", p=128)
                                        [:, 6 * sl:6 * sl + 6, dh * 512:(dh + 1) * 512],
                                        WF[par][:, 0:3072].rearrange("p (k c) -> p k c", k=6)),
                                    waits=(("SC", vc),), inc="din", inc_by=16)
                        P.res(f"WF{par}").addr("din", sp)
                        spills["wdt"] = sp
                        wd_d.append(("SC", vc))
                else:
                    def wd_slab_load(sl, dh=None):
                        p_ = sl % 2
                        dd = P.emit("sync",
                                    dma(WF[p_][:, 0:3072].rearrange("p (k c) -> p k c", k=6),
                                        wT_h["wdt"].rearrange("(k p) dd -> p k dd", p=128)
                                        [:, 6 * sl:6 * sl + 6, dh * 512:(dh + 1) * 512]),
                                    waits=P.wW(f"WF{p_}"),
                                    inc="din", inc_by=16)
                        P.res(f"WF{p_}").setw("din", dd)
                        wd_d.append(("din", dd))

                wd_slab_load(0, dh)
                wd_slab_load(1, dh)
                bank_w0 = {}
                for ssub in range(4):
                    c0, c1 = pool_cols[ssub]
                    bank_w0[ssub] = psw(c0, c1)[0]
                for hc in range(NHC):
                    sl, hcl = hc // 6, hc % 6
                    p_ = sl % 2
                    for ssub in range(4):
                        c0, c1 = pool_cols[ssub]
                        last_of_slab = (hcl == 5 and ssub == 3)
                        tl = P.emit(
                            "tensor",
                            lambda e, hc_=hc, hcl_=hcl, ss_=ssub, p__=p_, c0_=c0, c1_=c1: e.matmul(
                                psAll[:, c0_:c1_],
                                HT[:, hc_ * 512 + ss_ * 128: hc_ * 512 + (ss_ + 1) * 128],
                                WF[p__][:, hcl_ * 512:(hcl_ + 1) * 512],
                                start=(hc_ == 0), stop=(hc_ == NHC - 1)),
                            waits=((wd_d[sl], ("V", h_done_v)) + bank_w0[ssub])
                                  if hc == 0 else
                                  ((wd_d[sl],) if (hcl == 0 and ssub == 0) else ()),
                            inc="T" if last_of_slab else None)
                        if last_of_slab:
                            P.res(f"WF{p_}").addr("T", tl)
                            P.res("HT").addr("T", tl)
                            if sl + 2 < 4:
                                wd_slab_load(sl + 2, dh)
                t_down = P.cnt["T"]
                for ssub in range(4):
                    c0, c1 = pool_cols[ssub]
                    for b in psr(c0, c1):
                        P.res(b).setw("T", t_down)
                    ve = P.emit("vector",
                                lambda e, dh_=dh, ss_=ssub, c0_=c0, c1_=c1, xt_=xt: e.scalar_tensor_tensor(
                                    xt_[:, ss_ * 1024 + dh_ * 512: ss_ * 1024 + (dh_ + 1) * 512],
                                    psAll[:, c0_:c1_], DEQ2,
                                    xt_[:, ss_ * 1024 + dh_ * 512: ss_ * 1024 + (dh_ + 1) * 512],
                                    ALU.mult, ALU.add),
                                waits=(("T", t_down),), inc="V")
                    for b in psr(c0, c1):
                        P.res(b).addr("V", ve)
                    P.res(xres).setw("V", ve)
            last_ffn_evac_v = P.cnt["V"]

            if g + 1 == 1:
                wq_d_next = [wa_get("wqt", 0, g + 1), wa_get("wqt", 1, g + 1)]

            pending_store = (g, xt, xres, last_ffn_evac_v)
            if g == NG - 1:
                emit_store(pending_store)
                pending_store = None

        final_dout = P.cnt["dout"]
        P.emit("sync", lambda e: e.nop(), waits=(("dout", final_dout),))
        P.cut()

        block2 = nc.Block()
        with block2:
            for ename in P.ENGINES:
                lo, hi = P.section[ename][0], P.section[ename][1]
                ops = P.streams[ename][lo:hi]
                if not ops:
                    continue

                def body(eng, ops=ops):
                    for waits, fn, inc, inc_by in ops:
                        for s_, v_ in waits:
                            eng.wait_ge(semh[s_], v_)
                        inst = fn(eng)
                        if inc is not None:
                            inst.then_inc(semh[inc], inc_by)

                getattr(block2, ename)(body)

    return nc


_NC_CACHE = None


def _consts():
    ident = np.eye(128, dtype=ml_dtypes.bfloat16)
    tri = np.triu(np.ones((128, 128), dtype=np.float32)).astype(ml_dtypes.bfloat16)
    consts = np.zeros((128, 4), dtype=np.float32)
    consts[:, 0] = ESHIFT
    consts[:, 1] = EPS
    return ident, tri, consts


def kernel(L0_attn_in_features, L0_attn_w_q, L0_attn_w_k, L0_attn_w_v, L0_attn_w_o,
           L0_ffn_w_gate, L0_ffn_w_up, L0_ffn_w_down):
    global _NC_CACHE
    x = np.ascontiguousarray(np.asarray(L0_attn_in_features, dtype=np.float32))
    wq = np.ascontiguousarray(np.asarray(L0_attn_w_q, dtype=np.float32).T)
    wk = np.ascontiguousarray(np.asarray(L0_attn_w_k, dtype=np.float32).T)
    wv = np.ascontiguousarray(np.asarray(L0_attn_w_v, dtype=np.float32).T)
    wo = np.ascontiguousarray(np.asarray(L0_attn_w_o, dtype=np.float32).T)
    wg = np.ascontiguousarray(np.asarray(L0_ffn_w_gate, dtype=np.int32).T)
    wu = np.ascontiguousarray(np.asarray(L0_ffn_w_up, dtype=np.int32).T)
    wd = np.ascontiguousarray(np.asarray(L0_ffn_w_down, dtype=np.int32).T)
    ident, tri, consts = _consts()

    if _NC_CACHE is None:
        _NC_CACHE = build_nc()
    nc = _NC_CACHE

    in_maps = []
    for c in range(8):
        in_maps.append({
            "x": x[c], "wqt": wq, "wkt": wk, "wvt": wv, "wot": wo,
            "wgt": wg, "wut": wu, "wdt": wd,
            "ident": ident, "trimask": tri, "consts": consts,
        })
    res = run_bass_kernel_spmd(nc, in_maps, core_ids=list(range(8)))
    out = np.stack([res.results[c]["out"] for c in range(8)], axis=0)
    return out


def kernel_dbg(inputs, dbg_g=0):
    x = np.ascontiguousarray(np.asarray(inputs["L0_attn_in_features"], dtype=np.float32))
    ident, tri, consts = _consts()
    nc = build_nc(dbg=True, dbg_g=dbg_g)
    im = {
        "x": x[0],
        "wqt": np.ascontiguousarray(np.asarray(inputs["L0_attn_w_q"], dtype=np.float32).T),
        "wkt": np.ascontiguousarray(np.asarray(inputs["L0_attn_w_k"], dtype=np.float32).T),
        "wvt": np.ascontiguousarray(np.asarray(inputs["L0_attn_w_v"], dtype=np.float32).T),
        "wot": np.ascontiguousarray(np.asarray(inputs["L0_attn_w_o"], dtype=np.float32).T),
        "wgt": np.ascontiguousarray(np.asarray(inputs["L0_ffn_w_gate"], dtype=np.int32).T),
        "wut": np.ascontiguousarray(np.asarray(inputs["L0_ffn_w_up"], dtype=np.int32).T),
        "wdt": np.ascontiguousarray(np.asarray(inputs["L0_ffn_w_down"], dtype=np.int32).T),
        "ident": ident, "trimask": tri, "consts": consts,
    }
    res = run_bass_kernel_spmd(nc, [im], core_ids=[0])
    return res.results[0]


# version 41
# speedup vs baseline: 1.0087x; 1.0087x over previous
import numpy as np
import ml_dtypes
from contextlib import ExitStack

import concourse.bass as bass
import concourse.mybir as mybir
from concourse.bass_utils import run_bass_kernel_spmd

F32 = mybir.dt.float32
BF16 = mybir.dt.bfloat16
I32 = mybir.dt.int32
AF = mybir.ActivationFunctionType
ALU = mybir.AluOpType
AX = mybir.AxisListType.X

S, D, HID = 4096, 1024, 3072
NG, GT = 8, 512
NIC = D // 128
NHC = HID // 128
VHW = D + 1
EPS = 1e-5
ESHIFT = -16.0
QSCALE = 1.0 / 32.0
DEQ = 1.0 / 127.0
DEQ2 = DEQ * DEQ
INV_D = 1.0 / D


class Res:
    def __init__(self):
        self.w = None
        self.r = []

    def rd(self):
        return (self.w,) if self.w else ()

    def wr(self):
        out = list(self.r)
        if self.w:
            out.append(self.w)
        return tuple(out)

    def setw(self, sem, n):
        self.w = (sem, n)
        self.r = []

    def addr(self, sem, n):
        self.r.append((sem, n))


class Plan:
    ENGINES = ("sync", "tensor", "vector", "scalar", "gpsimd")
    SEM_OWNER = {"T": "tensor", "V": "vector", "SC": "scalar"}

    OVL = {
        "WF0": ["WA0", "WA1"],
        "WF1": ["WA1", "PT0", "PT1", "xsq"],
        "WA0": ["WF0"],
        "WA1": ["WF0", "WF1"],
        "PT0": ["WF1"],
        "PT1": ["WF1"],
        "xsq": ["WF1"],
        "HT": ["yT", "QT", "CT", "ztm0", "ztm1", "ztm2", "ztm3"],
        "yT": ["HT", "ztm0", "ztm1", "ztm2", "ztm3"],
        "QT": ["HT"],
        "CT": ["HT"],
        "ytm0": ["zT"], "ytm1": ["zT"], "ytm2": ["zT"], "ytm3": ["zT"],
        "zT": ["ytm0", "ytm1", "ytm2", "ytm3"],
        "ztm0": ["yT", "HT"], "ztm1": ["yT", "HT"],
        "ztm2": ["yT", "HT"], "ztm3": ["yT", "HT"],
        "sg": ["rrow", "bcsb"],
        "rrow": ["sg"], "bcsb": ["sg"],
        "rwg0": ["rwdA", "rwdB", "rwa0", "rwa1"],
        "rwg1": ["rwa1"],
        "rwdA": ["rwg0", "rwa0"], "rwdB": ["rwg0", "rwa0", "rwa1"],
        "rwa0": ["rwg0", "rwdA", "rwdB"],
        "rwa1": ["rwg0", "rwg1", "rwdB"],
    }

    def __init__(self):
        self.streams = {e: [] for e in self.ENGINES}
        self.cnt = {"T": 0, "V": 0, "SC": 0, "din": 0, "dout": 0}
        self.section = {e: [0] for e in self.ENGINES}
        self.R = {}

    def res(self, name):
        if name not in self.R:
            self.R[name] = Res()
        return self.R[name]

    def wW(self, name):
        s = list(self.res(name).wr())
        for o in self.OVL.get(name, ()):
            s += list(self.res(o).wr())
        return tuple(s)

    def wR(self, name):
        return self.res(name).rd()

    def cut(self):
        for e in self.ENGINES:
            self.section[e].append(len(self.streams[e]))

    def emit(self, eng, fn, waits=(), inc=None, inc_by=1):
        w = {}
        for s, v in waits:
            if v is None or v <= 0:
                continue
            assert v <= self.cnt[s], f"wait on future: {s} {v} > {self.cnt[s]} ({eng})"
            if self.SEM_OWNER.get(s) == eng:
                continue
            w[s] = max(w.get(s, 0), v)
        n = None
        if inc is not None:
            self.cnt[inc] += inc_by
            n = self.cnt[inc]
        self.streams[eng].append((tuple(w.items()), fn, inc, inc_by))
        return n


def build_nc(dbg=False, dbg_g=0):
    nc = bass.Bass("TRN2", target_bir_lowering=False, dynamic_dma_scratch_size=2048,
                   detect_race_conditions=False)

    x_h = nc.dram_tensor("x", [S, D], F32, kind="ExternalInput").ap()
    w_h = {}
    for nm in ("wqt", "wkt", "wvt", "wot"):
        w_h[nm] = nc.dram_tensor(nm, [D, D], F32, kind="ExternalInput").ap()
    w_h["wgt"] = nc.dram_tensor("wgt", [D, HID], I32, kind="ExternalInput").ap()
    w_h["wut"] = nc.dram_tensor("wut", [D, HID], I32, kind="ExternalInput").ap()
    w_h["wdt"] = nc.dram_tensor("wdt", [HID, D], I32, kind="ExternalInput").ap()
    ident_h = nc.dram_tensor("ident", [128, 128], BF16, kind="ExternalInput").ap()
    trimask_h = nc.dram_tensor("trimask", [128, 128], BF16, kind="ExternalInput").ap()
    consts_h = nc.dram_tensor("consts", [128, 4], F32, kind="ExternalInput").ap()
    out_h = nc.dram_tensor("out", [S, D], F32, kind="ExternalOutput").ap()

    dbg_h = {}
    if dbg:
        for nm, shp in (("d_stats", [128, 320]), ("d_yT", [128, 4096]),
                        ("d_QT", [128, 4096]), ("d_KT", [128, 4096]),
                        ("d_VH", [128, 1025]), ("d_PT", [128, 512]),
                        ("d_CT", [128, 4096]), ("d_x2", [128, 4096]),
                        ("d_zT", [128, 4096]), ("d_sig", [128, 512]),
                        ("d_HT", [128, 4096])):
            dbg_h[nm] = nc.dram_tensor(nm, shp, F32, kind="ExternalOutput").ap()

    wT_h = {
        "wqt": nc.dram_tensor("wqT", [D, D], BF16, kind="Internal").ap(),
        "wkt": nc.dram_tensor("wkT", [D, D], BF16, kind="Internal").ap(),
        "wvt": nc.dram_tensor("wvT", [D, D], BF16, kind="Internal").ap(),
        "wot": nc.dram_tensor("woT", [D, D], BF16, kind="Internal").ap(),
        "wgt": nc.dram_tensor("wgT", [D, HID], BF16, kind="Internal").ap(),
        "wut": nc.dram_tensor("wuT", [D, HID], BF16, kind="Internal").ap(),
        "wdt": nc.dram_tensor("wdT", [HID, D], BF16, kind="Internal").ap(),
    }

    P = Plan()

    def dma(dst, src):
        return lambda e, d=dst, s=src: e.dma_start(d, s)

    with ExitStack() as ctx:
        ident = ctx.enter_context(nc.sbuf_tensor("identS", [128, 128], BF16))
        trimask = ctx.enter_context(nc.sbuf_tensor("trimaskS", [128, 128], BF16))
        consts = ctx.enter_context(nc.sbuf_tensor("constsS", [128, 4], F32))
        ones_col = ctx.enter_context(nc.sbuf_tensor("ones_col", [128, 128], F32))

        semh = {}
        for s_ in ("din", "dout", "T", "V", "SC"):
            semh[s_] = ctx.enter_context(nc.semaphore(s_))

        KT = ctx.enter_context(nc.sbuf_tensor("KT", [128, NIC * S], BF16))
        VH = ctx.enter_context(nc.sbuf_tensor("VH", [128, 32 * VHW], BF16))
        xtA = ctx.enter_context(nc.sbuf_tensor("xtA", [128, 4096], F32))
        xtB = ctx.enter_context(nc.sbuf_tensor("xtB", [128, 4096], F32))
        xts = [xtA, xtB]
        zT = ctx.enter_context(nc.sbuf_tensor("zT", [128, 4096], BF16))
        R1 = ctx.enter_context(nc.sbuf_tensor("R1", [128, 12288], BF16))
        R2 = ctx.enter_context(nc.sbuf_tensor("R2", [128, 12288], BF16))
        small = ctx.enter_context(nc.sbuf_tensor("small", [128, 512], F32))
        sg = small[:, 0:512]
        stats = ctx.enter_context(nc.sbuf_tensor("stats", [128, 320], F32))

        psAll = ctx.enter_context(nc.psum_tensor("psAll", [128, 3584], F32))
        psT = ctx.enter_context(nc.psum_tensor("psT", [128, 1024], BF16))

        stIN = [KT[:, 16384:22528], KT[:, 22528:28672]]
        stBF = [VH[:, 16400:19472], VH[:, 19472:22544]]
        RWGU = [KT[:, 4096:16384].bitcast(I32), KT[:, 16384:28672].bitcast(I32)]
        RWD = [KT[:, 4096:10240].bitcast(I32), KT[:, 10240:16384].bitcast(I32)]
        spills = {}
        _prolog_par = [0]

        def ensure_weight(nm):
            if nm in spills:
                return
            src = w_h[nm]
            R, C = src.shape
            dt_in = src.dtype
            nb_max = 3072 // C
            r0 = 0
            sp = None
            while r0 < R // 128:
                nb = min(nb_max, R // 128 - r0)
                par = _prolog_par[0] % 2
                _prolog_par[0] += 1
                w_ = nb * C
                din_ap = stIN[par].bitcast(dt_in)
                dd = P.emit("sync",
                            dma(din_ap[:, 0:w_].rearrange("p (b c) -> p b c", b=nb),
                                src[r0 * 128:(r0 + nb) * 128, :]
                                .rearrange("(b p) c -> p b c", p=128)),
                            waits=P.wW(f"stin{par}"), inc="din", inc_by=16)
                P.res(f"stin{par}").setw("din", dd)
                vc = P.emit("vector",
                            lambda e, par_=par, w__=w_, da=din_ap: e.tensor_copy(
                                stBF[par_][:, 0:w__], da[:, 0:w__]),
                            waits=(("din", dd),) + P.wW(f"stbf{par}"), inc="V")
                P.res(f"stbf{par}").setw("V", vc)
                P.res(f"stin{par}").addr("V", vc)
                sp = P.emit("sync",
                            dma(wT_h[nm][r0 * 128:(r0 + nb) * 128, :]
                                .rearrange("(b p) c -> p b c", p=128),
                                stBF[par][:, 0:w_].rearrange("p (b c) -> p b c", b=nb)),
                            waits=(("V", vc),), inc="din", inc_by=16)
                P.res(f"stbf{par}").addr("din", sp)
                r0 += nb
            spills[nm] = sp

        yT = R1[:, 0:4096]
        QT = R1[:, 4096:8192]
        CT = R1[:, 8192:12288]
        HT = R1[:, 0:12288]
        WA = [R2[:, 0:4096], R2[:, 4096:8192]]
        PT = [R2[:, 8192:8704], R2[:, 8704:9216]]
        xsq = R2[:, 10240:11264]
        WF = [R2[:, 0:6144], R2[:, 6144:12288]]

        def psw(c0, c1):
            bks = [f"pb{b}" for b in range(c0 // 512, (c1 - 1) // 512 + 1)]
            w = ()
            for b in bks:
                w += P.wW(b)
            return w, bks

        def psr(c0, c1):
            return [f"pb{b}" for b in range(c0 // 512, (c1 - 1) // 512 + 1)]

        d_ident = P.emit("sync", dma(ident[:], ident_h), inc="din", inc_by=16)
        P.emit("sync", dma(trimask[:], trimask_h), inc="din", inc_by=16)
        d_consts = P.emit("sync", dma(consts[:], consts_h), inc="din", inc_by=16)
        v_ones = P.emit("vector", lambda e: e.memset(ones_col[:], 1.0),
                        waits=(("din", d_consts),), inc="V")

        prev_out_dout = 0
        last_ffn_evac_v = 0

        if dbg:
            dstage = ctx.enter_context(nc.sbuf_tensor("dstage", [128, 832], F32))
            P.res("dstage")

        def dbg_dump(nm, src_ap, waits, res_names=()):
            if not dbg:
                return
            shp = src_ap.shape
            p = shp[0]
            if len(shp) == 3:
                a, b = shp[1], shp[2]
                per = max(1, 832 // b)
                for a0 in range(0, a, per):
                    a1 = min(a0 + per, a)
                    w_ = b * (a1 - a0)
                    vv = P.emit("vector",
                                lambda e, a0_=a0, a1_=a1, w__=w_: e.tensor_copy(
                                    dstage[0:p, 0:w__].rearrange("p (aa bb) -> p aa bb", aa=a1_ - a0_),
                                    src_ap[:, a0_:a1_, :]),
                                waits=tuple(waits) + P.wW("dstage"), inc="V")
                    P.res("dstage").setw("V", vv)
                    for rn in res_names:
                        P.res(rn).addr("V", vv)
                    dd = P.emit("sync", dma(dbg_h[nm][0:p, a0 * b:a1 * b],
                                            dstage[0:p, 0:w_]),
                                waits=(("V", vv),), inc="dout", inc_by=16)
                    P.res("dstage").addr("dout", dd)
                return
            flat = shp[1]
            for c0 in range(0, flat, 832):
                c1 = min(c0 + 832, flat)
                vv = P.emit("vector",
                            lambda e, c0_=c0, c1_=c1: e.tensor_copy(
                                dstage[0:p, 0:c1_ - c0_], src_ap[:, c0_:c1_]),
                            waits=tuple(waits) + P.wW("dstage"), inc="V")
                P.res("dstage").setw("V", vv)
                for rn in res_names:
                    P.res(rn).addr("V", vv)
                dd = P.emit("sync", dma(dbg_h[nm][0:p, c0:c1], dstage[0:p, 0:c1 - c0]),
                            waits=(("V", vv),), inc="dout", inc_by=16)
                P.res("dstage").addr("dout", dd)

        def ln_stats_chunk(stats_base, xt, xres, t):
            st = stats[:, stats_base + t * 16: stats_base + t * 16 + 16]
            sres = f"st{stats_base}_{t}"
            P.res(sres)
            v = P.emit("vector",
                       lambda e, t_=t, st_=st, xt_=xt: e.scalar_tensor_tensor(
                           xsq[:, 0:1024], xt_[:, t_ * 1024:(t_ + 1) * 1024], 1.0,
                           xt_[:, t_ * 1024:(t_ + 1) * 1024], ALU.mult, ALU.mult,
                           accum_out=st_[:, 1:2]),
                       waits=P.wR(xres) + P.wW("xsq") + P.wW(sres), inc="V")
            P.res("xsq").setw("V", v)
            P.emit("vector",
                   lambda e, t_=t, st_=st, xt_=xt: e.reduce_sum(
                       st_[:, 0:1], xt_[:, t_ * 1024:(t_ + 1) * 1024], axis=AX))

        def ln_chain_apply(tm_prefix, tm_region, stats_base, xt):
            v_var = None
            for t in range(4):
                st = stats[:, stats_base + t * 16: stats_base + t * 16 + 16]
                P.emit("vector", lambda e, st_=st: e.tensor_scalar_mul(st_[:, 2:3], st_[:, 0:1], INV_D))
                P.emit("vector", lambda e, st_=st: e.tensor_scalar_mul(st_[:, 3:4], st_[:, 1:2], INV_D))
                v_var = P.emit("vector", lambda e, st_=st: e.tensor_scalar(
                    st_[:, 4:5], st_[:, 2:3], st_[:, 2:3], st_[:, 3:4],
                    ALU.mult, ALU.subtract), inc="V")
            s_sq = None
            for t in range(4):
                st = stats[:, stats_base + t * 16: stats_base + t * 16 + 16]
                s_sq = P.emit("scalar",
                              lambda e, st_=st: e.activation(st_[:, 6:7], st_[:, 4:5], AF.Sqrt,
                                                             scale=-1.0, bias=consts[:, 1:2]),
                              waits=(("V", v_var), ("din", d_consts)), inc="SC")
            for t in range(4):
                st = stats[:, stats_base + t * 16: stats_base + t * 16 + 16]
                P.emit("vector", lambda e, st_=st: e.reciprocal(st_[:, 7:8], st_[:, 6:7]),
                       waits=(("SC", s_sq),))
            v_rs = []
            for t in range(4):
                st = stats[:, stats_base + t * 16: stats_base + t * 16 + 16]
                v_rs.append(P.emit("vector",
                            lambda e, st_=st: e.scalar_tensor_tensor(
                                st_[:, 8:9], st_[:, 2:3], -1.0, st_[:, 7:8],
                                ALU.mult, ALU.mult),
                            inc="V"))
            sa_list = []
            for t in range(4):
                st = stats[:, stats_base + t * 16: stats_base + t * 16 + 16]
                tmr = f"{tm_prefix}{t}"
                tm = tm_region[:, t * 1024:(t + 1) * 1024]
                sa = P.emit("scalar",
                            lambda e, t_=t, st_=st, tm_=tm, xt_=xt: e.activation(
                                tm_, xt_[:, t_ * 1024:(t_ + 1) * 1024], AF.Identity,
                                scale=st_[:, 7:8], bias=st_[:, 8:9]),
                            waits=(("V", v_rs[t]),) + P.wW(tmr), inc="SC")
                P.res(tmr).setw("SC", sa)
                P.res(f"st{stats_base}_{t}").addr("SC", sa)
                sa_list.append(sa)
            return sa_list

        def emit_ln_pre(tm_prefix, tm_region, stats_base, xt, xres):
            for t in range(4):
                ln_stats_chunk(stats_base, xt, xres, t)
            return ln_chain_apply(tm_prefix, tm_region, stats_base, xt)

        def emit_ln_post(tm_prefix, tm_region, dst_rearr, dst_name, sa_list):
            for t in range(4):
                tmr = f"{tm_prefix}{t}"
                tm = tm_region[:, t * 1024:(t + 1) * 1024]
                sa = sa_list[t]
                tlast = None
                for ic in range(NIC):
                    tlast = P.emit("tensor",
                                   lambda e, ic_=ic, tm_=tm: e.transpose(
                                       psT[:, ic_ * 128:(ic_ + 1) * 128],
                                       tm_[:, ic_ * 128:(ic_ + 1) * 128], ident[:]),
                                   waits=(("SC", sa), ("din", d_ident)) +
                                         (P.wW("psT") if ic == 0 else ()),
                                   inc="T" if ic == NIC - 1 else None)
                P.res("psT").setw("T", tlast)
                P.res(tmr).addr("T", tlast)
                ve = P.emit("vector",
                            lambda e, t_=t: e.tensor_copy(
                                dst_rearr[:, :, t_ * 128:(t_ + 1) * 128],
                                psT[:].rearrange("p (k c) -> p k c", k=NIC)),
                            waits=(("T", tlast),) + (P.wW(dst_name) if t == 0 else ()),
                            inc="V")
                P.res("psT").addr("V", ve)
                P.res(dst_name).setw("V", ve)

        pool_cols = [(0, 512), (512, 1024), (1024, 1536), (1536, 2048)]

        dout_of_parity = [0, 0]

        def x_load(gg):
            par = gg % 2
            dx = P.emit("sync",
                        dma(xts[par][:].rearrange("p (t c) -> p t c", t=4),
                            x_h[gg * GT:(gg + 1) * GT, :].rearrange("(t p) c -> p t c", p=128)),
                        waits=(("dout", dout_of_parity[par]),) + P.wW(f"xt{par}"),
                        inc="din", inc_by=16)
            P.res(f"xt{par}").setw("din", dx)

        def wa_load2(nm, half, gg, extra=()):
            if gg == 0:
                ensure_weight(nm)
            dd = P.emit("sync",
                        dma(WA[half].rearrange("p (k c) -> p k c", k=NIC),
                            wT_h[nm].rearrange("(k p) o -> p k o", p=128)
                            [:, :, half * 512:(half + 1) * 512]),
                        waits=P.wW(f"WA{half}") + tuple(extra) +
                              ((("din", spills[nm]),) if gg == 0 else ()),
                        inc="din", inc_by=16)
            P.res(f"WA{half}").setw("din", dd)
            return dd

        RWA = [KT[:, 4096:12288].bitcast(F32), KT[:, 12288:20480].bitcast(F32)]
        _wa_par = [0]

        def wa_direct(nm, half, extra=()):
            par = _wa_par[0] % 2
            _wa_par[0] += 1
            rr = f"rwa{par}"
            dd = P.emit("sync",
                        dma(RWA[par].rearrange("p (k c) -> p k c", k=NIC),
                            w_h[nm].rearrange("(k p) o -> p k o", p=128)
                            [:, :, half * 512:(half + 1) * 512]),
                        waits=P.wW(rr), inc="din", inc_by=16)
            P.res(rr).setw("din", dd)
            vc = P.emit("vector",
                        lambda e, h_=half, par_=par: e.tensor_copy(
                            WA[h_][:, 0:4096], RWA[par_][:, 0:4096]),
                        waits=(("din", dd),) + P.wW(f"WA{half}") + tuple(extra),
                        inc="V")
            P.res(f"WA{half}").setw("V", vc)
            P.res(rr).addr("V", vc)
            sp = P.emit("sync",
                        dma(wT_h[nm].rearrange("(k p) o -> p k o", p=128)
                            [:, :, half * 512:(half + 1) * 512],
                            WA[half].rearrange("p (k c) -> p k c", k=NIC)),
                        waits=(("V", vc),), inc="din", inc_by=16)
            P.res(f"WA{half}").addr("din", sp)
            spills[nm] = sp
            return ("V", vc)

        def wa_get(nm, half, gg, extra=()):
            if gg == 0:
                return wa_direct(nm, half, extra)
            return ("din", wa_load2(nm, half, gg, extra))

        def stage_wq(ng):
            nonlocal wq_d_next
            kw = P.wW(f"wqsK{ng}")
            if ng in (2, 3):
                kw = kw + P.wW("rwg0")
            if ng in (4, 5, 6):
                kw = kw + P.wW("rwg1") + P.wW("stin0") + P.wW("stin1")
            dK = P.emit("sync",
                        dma(KT[:, 4096 * ng: 4096 * ng + 4096]
                            .rearrange("p (k c) -> p k c", k=NIC),
                            wT_h["wqt"].rearrange("(k p) o -> p k o", p=128)
                            [:, :, 0:512]),
                        waits=kw, inc="din", inc_by=16)
            P.res(f"wqsK{ng}").setw("din", dK)
            vw = P.wW(f"wqsV{ng}")
            if ng in (2, 3):
                vw = vw + P.wW("gu2") + P.wW("gu3")
            if ng in (4, 5):
                vw = vw + P.wW("stbf0") + P.wW("stbf1")
            dV = P.emit("sync",
                        dma(VH[:, 4100 * ng: 4100 * ng + 4096]
                            .rearrange("p (k c) -> p k c", k=NIC),
                            wT_h["wqt"].rearrange("(k p) o -> p k o", p=128)
                            [:, :, 512:1024]),
                        waits=vw, inc="din", inc_by=16)
            P.res(f"wqsV{ng}").setw("din", dV)
            wq_d_next = [("din", dK), ("din", dV)]

        def emit_store(ps_):
            gg, xt_, xres_, evac_v = ps_
            so = P.emit("sync",
                        dma(out_h[gg * GT:(gg + 1) * GT, :].rearrange("(t p) c -> p t c", p=128),
                            xt_[:].rearrange("p (t c) -> p t c", t=4)),
                        waits=(("V", evac_v),),
                        inc="dout", inc_by=16)
            P.res(xres_).addr("dout", so)
            dout_of_parity[gg % 2] = P.cnt["dout"]

        pending_store = None
        x_load(0)
        wq_d_next = [wa_get("wqt", 0, 0), wa_get("wqt", 1, 0)]
        ln1_sa_next = emit_ln_pre("ytm", zT, 0, xts[0], "xt0")

        for g in range(NG):
            xt = xts[g % 2]
            xres = f"xt{g % 2}"
            wq_d = wq_d_next
            ln1_sa = ln1_sa_next

            yT_r = yT.rearrange("p (k c) -> p k c", k=NIC)
            emit_ln_post("ytm", zT, yT_r, "yT", ln1_sa)
            if dbg and g == dbg_g:
                allw = (("V", P.cnt["V"]), ("SC", P.cnt["SC"]))
                dbg_dump("d_stats", stats[:], allw)
                dbg_dump("d_yT", yT, allw, ("yT",))

            pool_i = 0

            def qk_proj(w_d, evac, bufs=None, bufres=None):
                nonlocal pool_i
                if bufs is None:
                    bufs = (WA[0], WA[1])
                    bufres = ("WA0", "WA1")
                mm_done = [0, 0]
                for half in range(2):
                    for pl in range(4):
                        pc = half * 4 + pl
                        c0, c1 = pool_cols[pool_i % 4]; pool_i += 1
                        ps = psAll[:, c0:c1]
                        w0, bks = psw(c0, c1)
                        tlast = None
                        for ic in range(NIC):
                            tlast = P.emit(
                                "tensor",
                                lambda e, b_=bufs[half], pl_=pl, ic_=ic, ps_=ps: e.matmul(
                                    ps_,
                                    b_[:, ic_ * 512 + pl_ * 128: ic_ * 512 + (pl_ + 1) * 128],
                                    yT[:, ic_ * 512:(ic_ + 1) * 512],
                                    start=(ic_ == 0), stop=(ic_ == NIC - 1)),
                                waits=((w_d[half],) + P.wR("yT") + w0)
                                      if ic == 0 else (),
                                inc="T" if ic == NIC - 1 else None)
                        for b in bks:
                            P.res(b).setw("T", tlast)
                        P.res("yT").addr("T", tlast)
                        P.res(bufres[half]).addr("T", tlast)
                        sem, n = evac(pc, ps, tlast)
                        for b in bks:
                            P.res(b).addr(sem, n)
                    mm_done[half] = P.cnt["T"]
                return mm_done

            def q_evac(pc, ps, tlast):
                n = P.emit("scalar",
                           lambda e, pc_=pc, ps_=ps: e.activation(
                               QT[:, pc_ * 512:(pc_ + 1) * 512], ps_, AF.Copy, scale=QSCALE),
                           waits=(("T", tlast),) + (P.wW("QT") if pc == 0 else ()),
                           inc="SC")
                P.res("QT").setw("SC", n)
                return ("SC", n)

            if g >= 2:
                q_done = qk_proj(wq_d, q_evac,
                                 bufs=(KT[:, 4096 * g: 4096 * g + 4096],
                                       VH[:, 4100 * g: 4100 * g + 4096]),
                                 bufres=(f"wqsK{g}", f"wqsV{g}"))
                wk_d = [wa_get("wkt", 0, g), wa_get("wkt", 1, g)]
            else:
                q_done = qk_proj(wq_d, q_evac)
                wk_d = [wa_get("wkt", 0, g, (("T", q_done[0]),)),
                        wa_get("wkt", 1, g, (("T", q_done[1]),))]

            def k_evac(pc, ps, tlast):
                extra = ()
                if pc == 0:
                    extra = P.wW("KT")
                    if g in (1, 2, 3):
                        extra = extra + P.wW("rwg0")
                    if g in (4, 5, 6):
                        extra = extra + P.wW("stin0") + P.wW("stin1") + P.wW("rwg1")
                    if g >= 2:
                        extra = extra + P.wW(f"wqsK{g}")
                n = P.emit("vector",
                           lambda e, pc_=pc, ps_=ps, gg=g: e.tensor_copy(
                               KT[:, gg * 4096: (gg + 1) * 4096]
                               .rearrange("p (jj c) -> p jj c", jj=4)
                               [:, :, pc_ * 128:(pc_ + 1) * 128],
                               ps_.rearrange("p (jj c) -> p jj c", jj=4)),
                           waits=(("T", tlast),) + extra,
                           inc="V")
                P.res("KT").setw("V", n)
                return ("V", n)

            if pending_store is not None:
                emit_store(pending_store)
                pending_store = None
            k_done = qk_proj(wk_d, k_evac)
            wv_d = [wa_get("wvt", 0, g, (("T", k_done[0]),)),
                    wa_get("wvt", 1, g, (("T", k_done[1]),))]

            v_done = [0, 0]
            for dh in range(2):
                for ssub in range(4):
                    sidx = 4 * g + ssub
                    c0, c1 = pool_cols[pool_i % 4]; pool_i += 1
                    ps = psAll[:, c0:c1]
                    w0, bks = psw(c0, c1)
                    tlast = None
                    for ic in range(NIC):
                        tlast = P.emit(
                            "tensor",
                            lambda e, dh_=dh, ss_=ssub, ic_=ic, ps_=ps: e.matmul(
                                ps_,
                                yT[:, ic_ * 512 + ss_ * 128: ic_ * 512 + (ss_ + 1) * 128],
                                WA[dh_][:, ic_ * 512:(ic_ + 1) * 512],
                                start=(ic_ == 0), stop=(ic_ == NIC - 1)),
                            waits=((wv_d[dh],) + P.wR("yT") + w0)
                                  if ic == 0 else (),
                            inc="T" if ic == NIC - 1 else None)
                    for b in bks:
                        P.res(b).setw("T", tlast)
                    P.res("yT").addr("T", tlast)
                    P.res(f"WA{dh}").addr("T", tlast)
                    ve = P.emit("vector",
                                lambda e, si=sidx, dh_=dh, ps_=ps: e.tensor_copy(
                                    VH[:, si * VHW + dh_ * 512: si * VHW + (dh_ + 1) * 512],
                                    ps_),
                                waits=(("T", tlast),) +
                                      ((P.wW("VH") +
                                        (P.wW("stbf0") + P.wW("stbf1")
                                         if g in (4, 5) else ()) +
                                        (P.wW("gu2") if g in (1, 2) else ()) +
                                        (P.wW("gu3") if g in (2, 3) else ()) +
                                        (P.wW(f"wqsV{g}") if g >= 2 else ()))
                                       if (dh == 0 and ssub == 0) else ()),
                                inc="V")
                    for b in bks:
                        P.res(b).addr("V", ve)
                    P.res("VH").setw("V", ve)
                v_done[dh] = P.cnt["T"]
            vo = None
            for ssub in range(4):
                sidx = 4 * g + ssub
                vo = P.emit("vector",
                            lambda e, si=sidx: e.memset(
                                VH[:, si * VHW + 1024: si * VHW + 1025], 1.0),
                            inc="V")
            P.res("VH").setw("V", vo)

            wo_d = [wa_get("wot", 0, g, (("T", v_done[0]),)),
                    wa_get("wot", 1, g, (("T", v_done[1]),))]

            if dbg and g == dbg_g:
                allw = (("V", P.cnt["V"]), ("SC", P.cnt["SC"]))
                dbg_dump("d_QT", QT, allw, ("QT",))
                dbg_dump("d_KT", KT[:].rearrange("p (k c) -> p k c", k=NIC)[:, :, 0:512],
                         allw, ("KT",))
                dbg_dump("d_VH", VH[:, 0:VHW], allw, ("VH",))

            qs_ready = P.cnt["SC"]
            kv_ready = P.cnt["V"]
            ct_marks = []
            for qt in range(2):
                NSB = 4 * g + 2 * qt + 2
                NB = NSB // 2
                pv_last = None
                se_of = {}
                fix_of = {}

                def emit_scores(i, qt=qt, NB=NB):
                    par = i % 2
                    sc0 = 512 * par
                    w0, sbks = psw(sc0, sc0 + 512)
                    if qt == 0 and i == 0:
                        w0 = w0 + (("SC", qs_ready), ("V", kv_ready))
                    tlast = None
                    first = True
                    for jj in range(2):
                        j = 2 * i + jj
                        for ic in range(NIC):
                            tlast = P.emit(
                                "tensor",
                                lambda e, j_=j, jj_=jj, ic_=ic, sc0_=sc0, qt_=qt: e.matmul(
                                    psAll[:, sc0_ + jj_ * 256: sc0_ + (jj_ + 1) * 256],
                                    KT[:, j_ * 1024 + ic_ * 128: j_ * 1024 + (ic_ + 1) * 128],
                                    QT[:, ic_ * 512 + qt_ * 256: ic_ * 512 + qt_ * 256 + 256],
                                    start=(ic_ == 0), stop=(ic_ == NIC - 1)),
                                waits=tuple(w0) if first else (),
                                inc="T" if (jj == 1 and ic == NIC - 1) else None)
                            first = False
                    for b in sbks:
                        P.res(b).setw("T", tlast)
                    se = P.emit("scalar",
                                lambda e, sc0_=sc0, par_=par: e.activation(
                                    PT[par_], psAll[:, sc0_:sc0_ + 512], AF.Exp,
                                    bias=consts[:, 0:1]),
                                waits=(("T", tlast),) + P.wW(f"PT{par}"), inc="SC")
                    for b in sbks:
                        P.res(b).addr("SC", se)
                    P.res(f"PT{par}").setw("SC", se)
                    se_of[i] = se
                    vfix = None
                    for jj in range(2):
                        j = 2 * i + jj
                        b_ = j - (4 * g + 2 * qt)
                        if b_ < 0:
                            continue
                        if b_ > 0:
                            P.emit("vector",
                                   lambda e, par_=par, jj_=jj, b__=b_: e.memset(
                                       PT[par_][:, jj_ * 256: jj_ * 256 + b__ * 128], 0.0),
                                   waits=(("SC", se),))
                        vfix = P.emit("vector",
                                      lambda e, par_=par, jj_=jj, b__=b_: e.tensor_tensor(
                                          PT[par_][:, jj_ * 256 + b__ * 128: jj_ * 256 + (b__ + 1) * 128],
                                          PT[par_][:, jj_ * 256 + b__ * 128: jj_ * 256 + (b__ + 1) * 128],
                                          trimask[:], ALU.mult),
                                      waits=(("SC", se), ("din", d_consts)), inc="V")
                        P.res(f"PT{par}").addr("V", vfix)
                    if vfix is not None:
                        fix_of[i] = vfix
                    if dbg and g == dbg_g and qt == 0 and i == 0:
                        dbg_dump("d_PT", PT[0][:, 0:512],
                                 (("SC", se),) + ((("V", vfix),) if vfix else ()),
                                 ("PT0",))

                emit_scores(0)
                for i in range(NB):
                    if i + 1 < NB:
                        emit_scores(i + 1)
                    par = i % 2
                    se = se_of[i]
                    vfix = fix_of.get(i)
                    wpv = [("SC", se)]
                    if vfix is not None:
                        wpv.append(("V", vfix))
                    if i == 0:
                        wc, cbks = psw(1024, 3328)
                        wpv += list(wc) + list(P.wR("VH"))
                    else:
                        cbks = psr(1024, 3328)
                    tpv = None
                    firstpv = True
                    for jj in range(2):
                        j = 2 * i + jj
                        for dc in range(NIC):
                            tpv = P.emit(
                                "tensor",
                                lambda e, j_=j, jj_=jj, dc_=dc, par_=par, i_=i, NB_=NB: e.matmul(
                                    psAll[:, 1024 + dc_ * 256: 1024 + (dc_ + 1) * 256],
                                    VH[:, j_ * VHW + dc_ * 128: j_ * VHW + (dc_ + 1) * 128],
                                    PT[par_][:, jj_ * 256:(jj_ + 1) * 256],
                                    start=(i_ == 0 and jj_ == 0 and dc_ % 2 == 0),
                                    stop=(i_ == NB_ - 1 and jj_ == 1),
                                    skip_group_check=True),
                                waits=tuple(wpv) if firstpv else (),
                                inc=None)
                            firstpv = False
                        tpv = P.emit(
                            "tensor",
                            lambda e, j_=j, jj_=jj, par_=par, i_=i, NB_=NB: e.matmul(
                                psAll[0:1, 3072:3328],
                                VH[:, j_ * VHW + 1024: j_ * VHW + 1025],
                                PT[par_][:, jj_ * 256:(jj_ + 1) * 256],
                                start=(i_ == 0 and jj_ == 0),
                                stop=(i_ == NB_ - 1 and jj_ == 1),
                                skip_group_check=True),
                            inc="T" if jj == 1 else None)
                    for b in cbks:
                        P.res(b).setw("T", tpv)
                    P.res(f"PT{par}").addr("T", tpv)
                    pv_last = tpv
                vr = P.emit("vector",
                            lambda e: e.reciprocal(small[0:1, 0:256], psAll[0:1, 3072:3328]),
                            waits=(("T", pv_last),) + P.wW("rrow"), inc="V")
                P.res("rrow").setw("V", vr)
                for b in psr(3072, 3328):
                    P.res(b).addr("V", vr)
                tb = P.emit("tensor",
                            lambda e: e.matmul(psAll[:, 3328:3584], ones_col[0:1, :],
                                               small[0:1, 0:256], start=True, stop=True),
                            waits=(("V", vr), ("V", v_ones)), inc="T")
                P.res("pb6").setw("T", tb)
                vb = P.emit("vector",
                            lambda e: e.tensor_copy(small[:, 256:512], psAll[:, 3328:3584]),
                            waits=(("T", tb),) + P.wW("bcsb"), inc="V")
                P.res("pb6").addr("V", vb)
                P.res("bcsb").setw("V", vb)
                if dbg and g == dbg_g and qt == 0:
                    dbg_dump("d_sig", small[:, 0:512], (("V", vb),), ("bcsb", "rrow"))
                vm = None
                for dc in range(NIC):
                    vm = P.emit("vector",
                                lambda e, dc_=dc, qt_=qt: e.tensor_tensor(
                                    CT[:, dc_ * 512 + qt_ * 256: dc_ * 512 + qt_ * 256 + 256],
                                    psAll[:, 1024 + dc_ * 256: 1024 + (dc_ + 1) * 256],
                                    small[:, 256:512], ALU.mult),
                                waits=(("V", vb),) +
                                      (P.wW("CT") if (qt == 0 and dc == 0) else ()),
                                inc="V")
                    for b in psr(1024 + dc * 256, 1024 + (dc + 1) * 256):
                        P.res(b).addr("V", vm)
                P.res("bcsb").addr("V", vm)
                P.res("CT").setw("V", vm)
                ct_marks.append(("V", vm))

            P.res("QT").addr("T", P.cnt["T"])
            if dbg and g == dbg_g:
                dbg_dump("d_CT", CT, tuple(ct_marks), ("CT",))

            for ssub in range(4):
                for dh in range(2):
                    c0, c1 = pool_cols[pool_i % 4]; pool_i += 1
                    ps = psAll[:, c0:c1]
                    w0, bks = psw(c0, c1)
                    tlast = None
                    for ic in range(NIC):
                        tlast = P.emit(
                            "tensor",
                            lambda e, dh_=dh, ss_=ssub, ic_=ic, ps_=ps: e.matmul(
                                ps_,
                                CT[:, ic_ * 512 + ss_ * 128: ic_ * 512 + (ss_ + 1) * 128],
                                WA[dh_][:, ic_ * 512:(ic_ + 1) * 512],
                                start=(ic_ == 0), stop=(ic_ == NIC - 1)),
                            waits=(tuple(ct_marks) + (wo_d[dh],) + w0)
                                  if ic == 0 else (),
                            inc="T" if ic == NIC - 1 else None)
                    for b in bks:
                        P.res(b).setw("T", tlast)
                    P.res("CT").addr("T", tlast)
                    P.res(f"WA{dh}").addr("T", tlast)
                    ve = P.emit("vector",
                                lambda e, dh_=dh, ss_=ssub, ps_=ps, xt_=xt: e.scalar_tensor_tensor(
                                    xt_[:, ss_ * 1024 + dh_ * 512: ss_ * 1024 + (dh_ + 1) * 512],
                                    ps_, 1.0,
                                    xt_[:, ss_ * 1024 + dh_ * 512: ss_ * 1024 + (dh_ + 1) * 512],
                                    ALU.mult, ALU.add,
                                    accum_out=stats[:, 160 + ss_ * 16 + 9 + dh_:
                                                    160 + ss_ * 16 + 10 + dh_]),
                                waits=(("T", tlast),), inc="V")
                    for b in bks:
                        P.res(b).addr("V", ve)
                    P.res(xres).setw("V", ve)
                st_ = stats[:, 160 + ssub * 16: 160 + ssub * 16 + 16]
                P.res(f"st160_{ssub}")
                vq = P.emit("vector",
                            lambda e, t_=ssub, stx=st_, xt_=xt: e.scalar_tensor_tensor(
                                xsq[:, 0:1024], xt_[:, t_ * 1024:(t_ + 1) * 1024], 1.0,
                                xt_[:, t_ * 1024:(t_ + 1) * 1024], ALU.mult, ALU.mult,
                                accum_out=stx[:, 1:2]),
                            waits=P.wR(xres) + P.wW("xsq") + P.wW(f"st160_{ssub}"),
                            inc="V")
                P.res("xsq").setw("V", vq)
                P.emit("vector",
                       lambda e, stx=st_: e.tensor_scalar(
                           stx[:, 0:1], stx[:, 9:10], stx[:, 10:11], 1.0,
                           ALU.add, ALU.mult))

            if dbg and g == dbg_g:
                dbg_dump("d_x2", xt[:], (("V", P.cnt["V"]),), (xres,))

            zT_r = zT[:].rearrange("p (k c) -> p k c", k=NIC)
            ln2_sa = ln_chain_apply("ztm", HT, 160, xt)
            emit_ln_post("ztm", HT, zT_r, "zT", ln2_sa)
            if dbg and g == dbg_g:
                dbg_dump("d_zT", zT[:], (("V", P.cnt["V"]), ("SC", P.cnt["SC"])), ("zT",))

            gu_d = []

            if g == 0:
                SLW, HPS, NSL = 768, 6, 4
                gu_bufs = [(WF[0], "WF0", WF[1], "WF1"),
                           (VH[:, 4100:10244], "gu2", VH[:, 10244:16388], "gu3")]

                gu_pending = {}

                def gu_flush_spill(br):
                    nm, buf, sl_, vcm = gu_pending.pop(br)
                    sp = P.emit("sync",
                                dma(wT_h[nm].rearrange("(k p) hh -> p k hh", p=128)
                                    [:, :, sl_ * SLW:(sl_ + 1) * SLW],
                                    buf[:, 0:6144].rearrange("p (k c) -> p k c", k=NIC)),
                                waits=(vcm,), inc="din", inc_by=16)
                    P.res(br).addr("din", sp)
                    spills[nm] = sp

                def gu_slab_load(sl):
                    gb, gr, ub, ur = gu_bufs[sl % 2]
                    pair = []
                    for nm, buf, br, rw in (("wgt", gb, gr, 0), ("wut", ub, ur, 1)):
                        dd = P.emit("sync",
                                    dma(RWGU[rw].rearrange("p (k c) -> p k c", k=NIC),
                                        w_h[nm].rearrange("(k p) hh -> p k hh", p=128)
                                        [:, :, sl * SLW:(sl + 1) * SLW]),
                                    waits=P.wW(f"rwg{rw}") +
                                          ((P.wW("stin0") + P.wW("stin1"))
                                           if rw == 1 else ()),
                                    inc="din", inc_by=16)
                        P.res(f"rwg{rw}").setw("din", dd)
                        if br in gu_pending:
                            gu_flush_spill(br)
                        if sl < 2:
                            vc = P.emit("scalar",
                                        lambda e, b_=buf, rw_=rw: e.activation(
                                            b_[:, 0:6144], RWGU[rw_][:, 0:6144],
                                            AF.Copy),
                                        waits=(("din", dd),) + P.wW(br), inc="SC")
                            sem = "SC"
                        else:
                            vc = P.emit("vector",
                                        lambda e, b_=buf, rw_=rw: e.tensor_copy(
                                            b_[:, 0:6144], RWGU[rw_][:, 0:6144]),
                                        waits=(("din", dd),) + P.wW(br), inc="V")
                            sem = "V"
                        P.res(br).setw(sem, vc)
                        P.res(f"rwg{rw}").addr(sem, vc)
                        gu_pending[br] = (nm, buf, sl, (sem, vc))
                        pair.append((sem, vc))
                    gu_d.append(tuple(pair))

                def gu_lhs(sl, half, ic, hcl):
                    gb, gr, ub, ur = gu_bufs[sl % 2]
                    buf = gb if half == 0 else ub
                    return buf[:, ic * SLW + hcl * 128: ic * SLW + (hcl + 1) * 128]

                def gu_mark_readers(sl, tu):
                    gb, gr, ub, ur = gu_bufs[sl % 2]
                    P.res(gr).addr("T", tu)
                    P.res(ur).addr("T", tu)
            else:
                SLW, HPS, NSL = 384, 3, 8

                def gu_slab_load(sl):
                    p_ = sl % 2
                    wgd = P.emit("sync",
                                 dma(WF[p_][:, 0:3072].rearrange("p (k c) -> p k c", k=NIC),
                                     wT_h["wgt"].rearrange("(k p) hh -> p k hh", p=128)
                                     [:, :, sl * 384:(sl + 1) * 384]),
                                 waits=P.wW(f"WF{p_}"),
                                 inc="din", inc_by=16)
                    wud = P.emit("sync",
                                 dma(WF[p_][:, 3072:6144].rearrange("p (k c) -> p k c", k=NIC),
                                     wT_h["wut"].rearrange("(k p) hh -> p k hh", p=128)
                                     [:, :, sl * 384:(sl + 1) * 384]),
                                 inc="din", inc_by=16)
                    P.res(f"WF{p_}").setw("din", wud)
                    gu_d.append((("din", wgd), ("din", wud)))

                def gu_lhs(sl, half, ic, hcl):
                    p_ = sl % 2
                    base = half * 3072 + ic * SLW + hcl * 128
                    return WF[p_][:, base: base + 128]

                def gu_mark_readers(sl, tu):
                    P.res(f"WF{sl % 2}").addr("T", tu)

            gu_slab_load(0)
            gu_slab_load(1)

            if 1 <= g < NG - 1:
                x_load(g + 1)
                stage_wq(g + 1)

            GUC = [(2048, 2560), (2560, 3072)]
            h_done_v = 0
            for hc in range(NHC):
                sl, hcl = hc // HPS, hc % HPS
                wgw, wuw = gu_d[sl]
                gw, gbks = psw(*GUC[0])
                tg = None
                for ic in range(NIC):
                    tg = P.emit("tensor",
                                lambda e, ic_=ic, a_=gu_lhs(sl, 0, ic, hcl): e.matmul(
                                    psAll[:, GUC[0][0]:GUC[0][1]],
                                    a_,
                                    zT[:, ic_ * 512:(ic_ + 1) * 512],
                                    start=(ic_ == 0), stop=(ic_ == NIC - 1)),
                                waits=((wgw,) + P.wR("zT") + gw)
                                      if ic == 0 else (),
                                inc="T" if ic == NIC - 1 else None)
                for b in gbks:
                    P.res(b).setw("T", tg)
                P.res("zT").addr("T", tg)
                uw, ubks = psw(*GUC[1])
                tu = None
                for ic in range(NIC):
                    tu = P.emit("tensor",
                                lambda e, ic_=ic, a_=gu_lhs(sl, 1, ic, hcl): e.matmul(
                                    psAll[:, GUC[1][0]:GUC[1][1]],
                                    a_,
                                    zT[:, ic_ * 512:(ic_ + 1) * 512],
                                    start=(ic_ == 0), stop=(ic_ == NIC - 1)),
                                waits=((wuw,) + uw) if ic == 0 else (),
                                inc="T" if ic == NIC - 1 else None)
                for b in ubks:
                    P.res(b).setw("T", tu)
                if hcl == HPS - 1:
                    gu_mark_readers(sl, tu)
                    if sl + 2 < NSL:
                        gu_slab_load(sl + 2)
                ssil = P.emit("scalar",
                              lambda e: e.activation(sg[:], psAll[:, GUC[0][0]:GUC[0][1]],
                                                     AF.Silu, scale=DEQ),
                              waits=(("T", tg),) + P.wW("sg"), inc="SC")
                for b in gbks:
                    P.res(b).addr("SC", ssil)
                P.res("sg").setw("SC", ssil)
                vh_ = P.emit("vector",
                             lambda e, hc_=hc: e.tensor_tensor(
                                 HT[:, hc_ * 512:(hc_ + 1) * 512], sg[:],
                                 psAll[:, GUC[1][0]:GUC[1][1]], ALU.mult),
                             waits=(("SC", ssil), ("T", tu)) +
                                   (P.wW("HT") if hc == 0 else ()),
                             inc="V")
                for b in ubks:
                    P.res(b).addr("V", vh_)
                P.res("sg").addr("V", vh_)
                P.res("HT").setw("V", vh_)
                h_done_v = vh_

            if g == 0:
                for br_ in list(gu_pending):
                    gu_flush_spill(br_)

            if dbg and g == dbg_g:
                dbg_dump("d_HT", HT[:, 0:4096], (("V", P.cnt["V"]),), ("HT",))

            if g == 0 and g + 1 < NG:
                x_load(g + 1)
            if g + 1 < NG:
                ln1_sa_next = emit_ln_pre("ytm", zT, 0, xts[(g + 1) % 2],
                                          f"xt{(g + 1) % 2}")

            for dh in range(2):
                wd_d = []

                if g == 0:
                    def wd_slab_load(sl, dh=None):
                        par = sl % 2
                        rr = "rwdA" if par == 0 else "rwdB"
                        dd = P.emit("sync",
                                    dma(RWD[par].rearrange("p (k c) -> p k c", k=6),
                                        w_h["wdt"].rearrange("(k p) dd -> p k dd", p=128)
                                        [:, 6 * sl:6 * sl + 6, dh * 512:(dh + 1) * 512]),
                                    waits=P.wW(rr), inc="din", inc_by=16)
                        P.res(rr).setw("din", dd)
                        vc = P.emit("vector",
                                    lambda e, par_=par: e.tensor_copy(
                                        WF[par_][:, 0:3072], RWD[par_][:, 0:3072]),
                                    waits=(("din", dd),) + P.wW(f"WF{par}"), inc="V")
                        P.res(f"WF{par}").setw("V", vc)
                        P.res(rr).addr("V", vc)
                        sp = P.emit("sync",
                                    dma(wT_h["wdt"].rearrange("(k p) dd -> p k dd", p=128)
                                        [:, 6 * sl:6 * sl + 6, dh * 512:(dh + 1) * 512],
                                        WF[par][:, 0:3072].rearrange("p (k c) -> p k c", k=6)),
                                    waits=(("V", vc),), inc="din", inc_by=16)
                        P.res(f"WF{par}").addr("din", sp)
                        spills["wdt"] = sp
                        wd_d.append(("V", vc))
                else:
                    def wd_slab_load(sl, dh=None):
                        p_ = sl % 2
                        dd = P.emit("sync",
                                    dma(WF[p_][:, 0:3072].rearrange("p (k c) -> p k c", k=6),
                                        wT_h["wdt"].rearrange("(k p) dd -> p k dd", p=128)
                                        [:, 6 * sl:6 * sl + 6, dh * 512:(dh + 1) * 512]),
                                    waits=P.wW(f"WF{p_}"),
                                    inc="din", inc_by=16)
                        P.res(f"WF{p_}").setw("din", dd)
                        wd_d.append(("din", dd))

                wd_slab_load(0, dh)
                wd_slab_load(1, dh)
                bank_w0 = {}
                for ssub in range(4):
                    c0, c1 = pool_cols[ssub]
                    bank_w0[ssub] = psw(c0, c1)[0]
                for hc in range(NHC):
                    sl, hcl = hc // 6, hc % 6
                    p_ = sl % 2
                    for ssub in range(4):
                        c0, c1 = pool_cols[ssub]
                        last_of_slab = (hcl == 5 and ssub == 3)
                        tl = P.emit(
                            "tensor",
                            lambda e, hc_=hc, hcl_=hcl, ss_=ssub, p__=p_, c0_=c0, c1_=c1: e.matmul(
                                psAll[:, c0_:c1_],
                                HT[:, hc_ * 512 + ss_ * 128: hc_ * 512 + (ss_ + 1) * 128],
                                WF[p__][:, hcl_ * 512:(hcl_ + 1) * 512],
                                start=(hc_ == 0), stop=(hc_ == NHC - 1)),
                            waits=((wd_d[sl], ("V", h_done_v)) + bank_w0[ssub])
                                  if hc == 0 else
                                  ((wd_d[sl],) if (hcl == 0 and ssub == 0) else ()),
                            inc="T" if last_of_slab else None)
                        if last_of_slab:
                            P.res(f"WF{p_}").addr("T", tl)
                            P.res("HT").addr("T", tl)
                            if sl + 2 < 4:
                                wd_slab_load(sl + 2, dh)
                t_down = P.cnt["T"]
                for ssub in range(4):
                    c0, c1 = pool_cols[ssub]
                    for b in psr(c0, c1):
                        P.res(b).setw("T", t_down)
                    ve = P.emit("vector",
                                lambda e, dh_=dh, ss_=ssub, c0_=c0, c1_=c1, xt_=xt: e.scalar_tensor_tensor(
                                    xt_[:, ss_ * 1024 + dh_ * 512: ss_ * 1024 + (dh_ + 1) * 512],
                                    psAll[:, c0_:c1_], DEQ2,
                                    xt_[:, ss_ * 1024 + dh_ * 512: ss_ * 1024 + (dh_ + 1) * 512],
                                    ALU.mult, ALU.add),
                                waits=(("T", t_down),), inc="V")
                    for b in psr(c0, c1):
                        P.res(b).addr("V", ve)
                    P.res(xres).setw("V", ve)
            last_ffn_evac_v = P.cnt["V"]

            if g + 1 == 1:
                wq_d_next = [wa_get("wqt", 0, g + 1), wa_get("wqt", 1, g + 1)]

            pending_store = (g, xt, xres, last_ffn_evac_v)
            if g == NG - 1:
                emit_store(pending_store)
                pending_store = None

        final_dout = P.cnt["dout"]
        P.emit("sync", lambda e: e.nop(), waits=(("dout", final_dout),))
        P.cut()

        block2 = nc.Block()
        with block2:
            for ename in P.ENGINES:
                lo, hi = P.section[ename][0], P.section[ename][1]
                ops = P.streams[ename][lo:hi]
                if not ops:
                    continue

                def body(eng, ops=ops):
                    for waits, fn, inc, inc_by in ops:
                        for s_, v_ in waits:
                            eng.wait_ge(semh[s_], v_)
                        inst = fn(eng)
                        if inc is not None:
                            inst.then_inc(semh[inc], inc_by)

                getattr(block2, ename)(body)

    return nc


_NC_CACHE = None


def _consts():
    ident = np.eye(128, dtype=ml_dtypes.bfloat16)
    tri = np.triu(np.ones((128, 128), dtype=np.float32)).astype(ml_dtypes.bfloat16)
    consts = np.zeros((128, 4), dtype=np.float32)
    consts[:, 0] = ESHIFT
    consts[:, 1] = EPS
    return ident, tri, consts


def kernel(L0_attn_in_features, L0_attn_w_q, L0_attn_w_k, L0_attn_w_v, L0_attn_w_o,
           L0_ffn_w_gate, L0_ffn_w_up, L0_ffn_w_down):
    global _NC_CACHE
    x = np.ascontiguousarray(np.asarray(L0_attn_in_features, dtype=np.float32))
    wq = np.ascontiguousarray(np.asarray(L0_attn_w_q, dtype=np.float32).T)
    wk = np.ascontiguousarray(np.asarray(L0_attn_w_k, dtype=np.float32).T)
    wv = np.ascontiguousarray(np.asarray(L0_attn_w_v, dtype=np.float32).T)
    wo = np.ascontiguousarray(np.asarray(L0_attn_w_o, dtype=np.float32).T)
    wg = np.ascontiguousarray(np.asarray(L0_ffn_w_gate, dtype=np.int32).T)
    wu = np.ascontiguousarray(np.asarray(L0_ffn_w_up, dtype=np.int32).T)
    wd = np.ascontiguousarray(np.asarray(L0_ffn_w_down, dtype=np.int32).T)
    ident, tri, consts = _consts()

    if _NC_CACHE is None:
        _NC_CACHE = build_nc()
    nc = _NC_CACHE

    in_maps = []
    for c in range(8):
        in_maps.append({
            "x": x[c], "wqt": wq, "wkt": wk, "wvt": wv, "wot": wo,
            "wgt": wg, "wut": wu, "wdt": wd,
            "ident": ident, "trimask": tri, "consts": consts,
        })
    res = run_bass_kernel_spmd(nc, in_maps, core_ids=list(range(8)))
    out = np.stack([res.results[c]["out"] for c in range(8)], axis=0)
    return out


def kernel_dbg(inputs, dbg_g=0):
    x = np.ascontiguousarray(np.asarray(inputs["L0_attn_in_features"], dtype=np.float32))
    ident, tri, consts = _consts()
    nc = build_nc(dbg=True, dbg_g=dbg_g)
    im = {
        "x": x[0],
        "wqt": np.ascontiguousarray(np.asarray(inputs["L0_attn_w_q"], dtype=np.float32).T),
        "wkt": np.ascontiguousarray(np.asarray(inputs["L0_attn_w_k"], dtype=np.float32).T),
        "wvt": np.ascontiguousarray(np.asarray(inputs["L0_attn_w_v"], dtype=np.float32).T),
        "wot": np.ascontiguousarray(np.asarray(inputs["L0_attn_w_o"], dtype=np.float32).T),
        "wgt": np.ascontiguousarray(np.asarray(inputs["L0_ffn_w_gate"], dtype=np.int32).T),
        "wut": np.ascontiguousarray(np.asarray(inputs["L0_ffn_w_up"], dtype=np.int32).T),
        "wdt": np.ascontiguousarray(np.asarray(inputs["L0_ffn_w_down"], dtype=np.int32).T),
        "ident": ident, "trimask": tri, "consts": consts,
    }
    res = run_bass_kernel_spmd(nc, [im], core_ids=[0])
    return res.results[0]


# version 42
# speedup vs baseline: 1.0135x; 1.0047x over previous
import numpy as np
import ml_dtypes
from contextlib import ExitStack

import concourse.bass as bass
import concourse.mybir as mybir
from concourse.bass_utils import run_bass_kernel_spmd

F32 = mybir.dt.float32
BF16 = mybir.dt.bfloat16
I32 = mybir.dt.int32
AF = mybir.ActivationFunctionType
ALU = mybir.AluOpType
AX = mybir.AxisListType.X

S, D, HID = 4096, 1024, 3072
NG, GT = 8, 512
NIC = D // 128
NHC = HID // 128
VHW = D + 1
EPS = 1e-5
ESHIFT = -16.0
QSCALE = 1.0 / 32.0
DEQ = 1.0 / 127.0
DEQ2 = DEQ * DEQ
INV_D = 1.0 / D


class Res:
    def __init__(self):
        self.w = None
        self.r = []

    def rd(self):
        return (self.w,) if self.w else ()

    def wr(self):
        out = list(self.r)
        if self.w:
            out.append(self.w)
        return tuple(out)

    def setw(self, sem, n):
        self.w = (sem, n)
        self.r = []

    def addr(self, sem, n):
        self.r.append((sem, n))


class Plan:
    ENGINES = ("sync", "tensor", "vector", "scalar", "gpsimd")
    SEM_OWNER = {"T": "tensor", "V": "vector", "SC": "scalar"}

    OVL = {
        "WF0": ["WA0", "WA1"],
        "WF1": ["WA1", "PT0", "PT1", "xsq"],
        "WA0": ["WF0"],
        "WA1": ["WF0", "WF1"],
        "PT0": ["WF1"],
        "PT1": ["WF1"],
        "xsq": ["WF1"],
        "HT": ["yT", "QT", "CT", "ztm0", "ztm1", "ztm2", "ztm3"],
        "yT": ["HT", "ztm0", "ztm1", "ztm2", "ztm3"],
        "QT": ["HT"],
        "CT": ["HT"],
        "ytm0": ["zT"], "ytm1": ["zT"], "ytm2": ["zT"], "ytm3": ["zT"],
        "zT": ["ytm0", "ytm1", "ytm2", "ytm3"],
        "ztm0": ["yT", "HT"], "ztm1": ["yT", "HT"],
        "ztm2": ["yT", "HT"], "ztm3": ["yT", "HT"],
        "sg": ["rrow", "bcsb"],
        "rrow": ["sg"], "bcsb": ["sg"],
        "rwg0": ["rwdA", "rwdB", "rwa0", "rwa1"],
        "rwg1": ["rwa1"],
        "rwdA": ["rwg0", "rwa0"], "rwdB": ["rwg0", "rwa0", "rwa1"],
        "rwa0": ["rwg0", "rwdA", "rwdB"],
        "rwa1": ["rwg0", "rwg1", "rwdB"],
    }

    def __init__(self):
        self.streams = {e: [] for e in self.ENGINES}
        self.cnt = {"T": 0, "V": 0, "SC": 0, "din": 0, "dout": 0}
        self.section = {e: [0] for e in self.ENGINES}
        self.R = {}

    def res(self, name):
        if name not in self.R:
            self.R[name] = Res()
        return self.R[name]

    def wW(self, name):
        s = list(self.res(name).wr())
        for o in self.OVL.get(name, ()):
            s += list(self.res(o).wr())
        return tuple(s)

    def wR(self, name):
        return self.res(name).rd()

    def cut(self):
        for e in self.ENGINES:
            self.section[e].append(len(self.streams[e]))

    def emit(self, eng, fn, waits=(), inc=None, inc_by=1):
        w = {}
        for s, v in waits:
            if v is None or v <= 0:
                continue
            assert v <= self.cnt[s], f"wait on future: {s} {v} > {self.cnt[s]} ({eng})"
            if self.SEM_OWNER.get(s) == eng:
                continue
            w[s] = max(w.get(s, 0), v)
        n = None
        if inc is not None:
            self.cnt[inc] += inc_by
            n = self.cnt[inc]
        self.streams[eng].append((tuple(w.items()), fn, inc, inc_by))
        return n


def build_nc(dbg=False, dbg_g=0):
    nc = bass.Bass("TRN2", target_bir_lowering=False, dynamic_dma_scratch_size=2048,
                   detect_race_conditions=False)

    x_h = nc.dram_tensor("x", [S, D], F32, kind="ExternalInput").ap()
    w_h = {}
    for nm in ("wqt", "wkt", "wvt", "wot"):
        w_h[nm] = nc.dram_tensor(nm, [D, D], F32, kind="ExternalInput").ap()
    w_h["wgt"] = nc.dram_tensor("wgt", [D, HID], I32, kind="ExternalInput").ap()
    w_h["wut"] = nc.dram_tensor("wut", [D, HID], I32, kind="ExternalInput").ap()
    w_h["wdt"] = nc.dram_tensor("wdt", [HID, D], I32, kind="ExternalInput").ap()
    ident_h = nc.dram_tensor("ident", [128, 128], BF16, kind="ExternalInput").ap()
    trimask_h = nc.dram_tensor("trimask", [128, 128], BF16, kind="ExternalInput").ap()
    consts_h = nc.dram_tensor("consts", [128, 4], F32, kind="ExternalInput").ap()
    out_h = nc.dram_tensor("out", [S, D], F32, kind="ExternalOutput").ap()

    dbg_h = {}
    if dbg:
        for nm, shp in (("d_stats", [128, 320]), ("d_yT", [128, 4096]),
                        ("d_QT", [128, 4096]), ("d_KT", [128, 4096]),
                        ("d_VH", [128, 1025]), ("d_PT", [128, 512]),
                        ("d_CT", [128, 4096]), ("d_x2", [128, 4096]),
                        ("d_zT", [128, 4096]), ("d_sig", [128, 512]),
                        ("d_HT", [128, 4096])):
            dbg_h[nm] = nc.dram_tensor(nm, shp, F32, kind="ExternalOutput").ap()

    wT_h = {
        "wqt": nc.dram_tensor("wqT", [D, D], BF16, kind="Internal").ap(),
        "wkt": nc.dram_tensor("wkT", [D, D], BF16, kind="Internal").ap(),
        "wvt": nc.dram_tensor("wvT", [D, D], BF16, kind="Internal").ap(),
        "wot": nc.dram_tensor("woT", [D, D], BF16, kind="Internal").ap(),
        "wgt": nc.dram_tensor("wgT", [D, HID], BF16, kind="Internal").ap(),
        "wut": nc.dram_tensor("wuT", [D, HID], BF16, kind="Internal").ap(),
        "wdt": nc.dram_tensor("wdT", [HID, D], BF16, kind="Internal").ap(),
    }

    P = Plan()

    def dma(dst, src):
        return lambda e, d=dst, s=src: e.dma_start(d, s)

    with ExitStack() as ctx:
        ident = ctx.enter_context(nc.sbuf_tensor("identS", [128, 128], BF16))
        trimask = ctx.enter_context(nc.sbuf_tensor("trimaskS", [128, 128], BF16))
        consts = ctx.enter_context(nc.sbuf_tensor("constsS", [128, 4], F32))
        ones_col = ctx.enter_context(nc.sbuf_tensor("ones_col", [128, 128], F32))

        semh = {}
        for s_ in ("din", "dout", "T", "V", "SC"):
            semh[s_] = ctx.enter_context(nc.semaphore(s_))

        KT = ctx.enter_context(nc.sbuf_tensor("KT", [128, NIC * S], BF16))
        VH = ctx.enter_context(nc.sbuf_tensor("VH", [128, 32 * VHW], BF16))
        xtA = ctx.enter_context(nc.sbuf_tensor("xtA", [128, 4096], F32))
        xtB = ctx.enter_context(nc.sbuf_tensor("xtB", [128, 4096], F32))
        xts = [xtA, xtB]
        zT = ctx.enter_context(nc.sbuf_tensor("zT", [128, 4096], BF16))
        R1 = ctx.enter_context(nc.sbuf_tensor("R1", [128, 12288], BF16))
        R2 = ctx.enter_context(nc.sbuf_tensor("R2", [128, 12288], BF16))
        small = ctx.enter_context(nc.sbuf_tensor("small", [128, 512], F32))
        sg = small[:, 0:512]
        stats = ctx.enter_context(nc.sbuf_tensor("stats", [128, 320], F32))

        psAll = ctx.enter_context(nc.psum_tensor("psAll", [128, 3584], F32))
        psT = ctx.enter_context(nc.psum_tensor("psT", [128, 1024], BF16))

        stIN = [KT[:, 16384:22528], KT[:, 22528:28672]]
        stBF = [VH[:, 16400:19472], VH[:, 19472:22544]]
        RWGU = [KT[:, 4096:16384].bitcast(I32), KT[:, 16384:28672].bitcast(I32)]
        RWD = [KT[:, 4096:10240].bitcast(I32), KT[:, 10240:16384].bitcast(I32)]
        spills = {}
        _prolog_par = [0]

        def ensure_weight(nm):
            if nm in spills:
                return
            src = w_h[nm]
            R, C = src.shape
            dt_in = src.dtype
            nb_max = 3072 // C
            r0 = 0
            sp = None
            while r0 < R // 128:
                nb = min(nb_max, R // 128 - r0)
                par = _prolog_par[0] % 2
                _prolog_par[0] += 1
                w_ = nb * C
                din_ap = stIN[par].bitcast(dt_in)
                dd = P.emit("sync",
                            dma(din_ap[:, 0:w_].rearrange("p (b c) -> p b c", b=nb),
                                src[r0 * 128:(r0 + nb) * 128, :]
                                .rearrange("(b p) c -> p b c", p=128)),
                            waits=P.wW(f"stin{par}"), inc="din", inc_by=16)
                P.res(f"stin{par}").setw("din", dd)
                vc = P.emit("vector",
                            lambda e, par_=par, w__=w_, da=din_ap: e.tensor_copy(
                                stBF[par_][:, 0:w__], da[:, 0:w__]),
                            waits=(("din", dd),) + P.wW(f"stbf{par}"), inc="V")
                P.res(f"stbf{par}").setw("V", vc)
                P.res(f"stin{par}").addr("V", vc)
                sp = P.emit("sync",
                            dma(wT_h[nm][r0 * 128:(r0 + nb) * 128, :]
                                .rearrange("(b p) c -> p b c", p=128),
                                stBF[par][:, 0:w_].rearrange("p (b c) -> p b c", b=nb)),
                            waits=(("V", vc),), inc="din", inc_by=16)
                P.res(f"stbf{par}").addr("din", sp)
                r0 += nb
            spills[nm] = sp

        yT = R1[:, 0:4096]
        QT = R1[:, 4096:8192]
        CT = R1[:, 8192:12288]
        HT = R1[:, 0:12288]
        WA = [R2[:, 0:4096], R2[:, 4096:8192]]
        PT = [R2[:, 8192:8704], R2[:, 8704:9216]]
        xsq = R2[:, 10240:11264]
        WF = [R2[:, 0:6144], R2[:, 6144:12288]]

        def psw(c0, c1):
            bks = [f"pb{b}" for b in range(c0 // 512, (c1 - 1) // 512 + 1)]
            w = ()
            for b in bks:
                w += P.wW(b)
            return w, bks

        def psr(c0, c1):
            return [f"pb{b}" for b in range(c0 // 512, (c1 - 1) // 512 + 1)]

        d_ident = P.emit("sync", dma(ident[:], ident_h), inc="din", inc_by=16)
        P.emit("sync", dma(trimask[:], trimask_h), inc="din", inc_by=16)
        d_consts = P.emit("sync", dma(consts[:], consts_h), inc="din", inc_by=16)
        v_ones = P.emit("vector", lambda e: e.memset(ones_col[:], 1.0),
                        waits=(("din", d_consts),), inc="V")

        prev_out_dout = 0
        last_ffn_evac_v = 0

        if dbg:
            dstage = ctx.enter_context(nc.sbuf_tensor("dstage", [128, 832], F32))
            P.res("dstage")

        def dbg_dump(nm, src_ap, waits, res_names=()):
            if not dbg:
                return
            shp = src_ap.shape
            p = shp[0]
            if len(shp) == 3:
                a, b = shp[1], shp[2]
                per = max(1, 832 // b)
                for a0 in range(0, a, per):
                    a1 = min(a0 + per, a)
                    w_ = b * (a1 - a0)
                    vv = P.emit("vector",
                                lambda e, a0_=a0, a1_=a1, w__=w_: e.tensor_copy(
                                    dstage[0:p, 0:w__].rearrange("p (aa bb) -> p aa bb", aa=a1_ - a0_),
                                    src_ap[:, a0_:a1_, :]),
                                waits=tuple(waits) + P.wW("dstage"), inc="V")
                    P.res("dstage").setw("V", vv)
                    for rn in res_names:
                        P.res(rn).addr("V", vv)
                    dd = P.emit("sync", dma(dbg_h[nm][0:p, a0 * b:a1 * b],
                                            dstage[0:p, 0:w_]),
                                waits=(("V", vv),), inc="dout", inc_by=16)
                    P.res("dstage").addr("dout", dd)
                return
            flat = shp[1]
            for c0 in range(0, flat, 832):
                c1 = min(c0 + 832, flat)
                vv = P.emit("vector",
                            lambda e, c0_=c0, c1_=c1: e.tensor_copy(
                                dstage[0:p, 0:c1_ - c0_], src_ap[:, c0_:c1_]),
                            waits=tuple(waits) + P.wW("dstage"), inc="V")
                P.res("dstage").setw("V", vv)
                for rn in res_names:
                    P.res(rn).addr("V", vv)
                dd = P.emit("sync", dma(dbg_h[nm][0:p, c0:c1], dstage[0:p, 0:c1 - c0]),
                            waits=(("V", vv),), inc="dout", inc_by=16)
                P.res("dstage").addr("dout", dd)

        def ln_stats_chunk(stats_base, xt, xres, t):
            st = stats[:, stats_base + t * 16: stats_base + t * 16 + 16]
            sres = f"st{stats_base}_{t}"
            P.res(sres)
            v = P.emit("vector",
                       lambda e, t_=t, st_=st, xt_=xt: e.scalar_tensor_tensor(
                           xsq[:, 0:1024], xt_[:, t_ * 1024:(t_ + 1) * 1024], 1.0,
                           xt_[:, t_ * 1024:(t_ + 1) * 1024], ALU.mult, ALU.mult,
                           accum_out=st_[:, 1:2]),
                       waits=P.wR(xres) + P.wW("xsq") + P.wW(sres), inc="V")
            P.res("xsq").setw("V", v)
            P.emit("vector",
                   lambda e, t_=t, st_=st, xt_=xt: e.reduce_sum(
                       st_[:, 0:1], xt_[:, t_ * 1024:(t_ + 1) * 1024], axis=AX))

        def ln_chain_apply(tm_prefix, tm_region, stats_base, xt):
            v_var = None
            for t in range(4):
                st = stats[:, stats_base + t * 16: stats_base + t * 16 + 16]
                P.emit("vector", lambda e, st_=st: e.tensor_scalar_mul(st_[:, 2:3], st_[:, 0:1], INV_D))
                P.emit("vector", lambda e, st_=st: e.tensor_scalar_mul(st_[:, 3:4], st_[:, 1:2], INV_D))
                v_var = P.emit("vector", lambda e, st_=st: e.tensor_scalar(
                    st_[:, 4:5], st_[:, 2:3], st_[:, 2:3], st_[:, 3:4],
                    ALU.mult, ALU.subtract), inc="V")
            s_sq = None
            for t in range(4):
                st = stats[:, stats_base + t * 16: stats_base + t * 16 + 16]
                s_sq = P.emit("scalar",
                              lambda e, st_=st: e.activation(st_[:, 6:7], st_[:, 4:5], AF.Sqrt,
                                                             scale=-1.0, bias=consts[:, 1:2]),
                              waits=(("V", v_var), ("din", d_consts)), inc="SC")
            for t in range(4):
                st = stats[:, stats_base + t * 16: stats_base + t * 16 + 16]
                P.emit("vector", lambda e, st_=st: e.reciprocal(st_[:, 7:8], st_[:, 6:7]),
                       waits=(("SC", s_sq),))
            v_rs = []
            for t in range(4):
                st = stats[:, stats_base + t * 16: stats_base + t * 16 + 16]
                v_rs.append(P.emit("vector",
                            lambda e, st_=st: e.scalar_tensor_tensor(
                                st_[:, 8:9], st_[:, 2:3], -1.0, st_[:, 7:8],
                                ALU.mult, ALU.mult),
                            inc="V"))
            sa_list = []
            for t in range(4):
                st = stats[:, stats_base + t * 16: stats_base + t * 16 + 16]
                tmr = f"{tm_prefix}{t}"
                tm = tm_region[:, t * 1024:(t + 1) * 1024]
                sa = P.emit("scalar",
                            lambda e, t_=t, st_=st, tm_=tm, xt_=xt: e.activation(
                                tm_, xt_[:, t_ * 1024:(t_ + 1) * 1024], AF.Identity,
                                scale=st_[:, 7:8], bias=st_[:, 8:9]),
                            waits=(("V", v_rs[t]),) + P.wW(tmr), inc="SC")
                P.res(tmr).setw("SC", sa)
                P.res(f"st{stats_base}_{t}").addr("SC", sa)
                sa_list.append(sa)
            return sa_list

        def emit_ln_pre(tm_prefix, tm_region, stats_base, xt, xres):
            for t in range(4):
                ln_stats_chunk(stats_base, xt, xres, t)
            return ln_chain_apply(tm_prefix, tm_region, stats_base, xt)

        def emit_ln_post(tm_prefix, tm_region, dst_rearr, dst_name, sa_list):
            for t in range(4):
                tmr = f"{tm_prefix}{t}"
                tm = tm_region[:, t * 1024:(t + 1) * 1024]
                sa = sa_list[t]
                tlast = None
                for ic in range(NIC):
                    tlast = P.emit("tensor",
                                   lambda e, ic_=ic, tm_=tm: e.transpose(
                                       psT[:, ic_ * 128:(ic_ + 1) * 128],
                                       tm_[:, ic_ * 128:(ic_ + 1) * 128], ident[:]),
                                   waits=(("SC", sa), ("din", d_ident)) +
                                         (P.wW("psT") if ic == 0 else ()),
                                   inc="T" if ic == NIC - 1 else None)
                P.res("psT").setw("T", tlast)
                P.res(tmr).addr("T", tlast)
                ve = P.emit("vector",
                            lambda e, t_=t: e.tensor_copy(
                                dst_rearr[:, :, t_ * 128:(t_ + 1) * 128],
                                psT[:].rearrange("p (k c) -> p k c", k=NIC)),
                            waits=(("T", tlast),) + (P.wW(dst_name) if t == 0 else ()),
                            inc="V")
                P.res("psT").addr("V", ve)
                P.res(dst_name).setw("V", ve)

        pool_cols = [(0, 512), (512, 1024), (1024, 1536), (1536, 2048)]

        dout_of_parity = [0, 0]

        def x_load(gg):
            par = gg % 2
            dx = P.emit("sync",
                        dma(xts[par][:].rearrange("p (t c) -> p t c", t=4),
                            x_h[gg * GT:(gg + 1) * GT, :].rearrange("(t p) c -> p t c", p=128)),
                        waits=(("dout", dout_of_parity[par]),) + P.wW(f"xt{par}"),
                        inc="din", inc_by=16)
            P.res(f"xt{par}").setw("din", dx)

        def wa_load2(nm, half, gg, extra=()):
            if gg == 0:
                ensure_weight(nm)
            dd = P.emit("sync",
                        dma(WA[half].rearrange("p (k c) -> p k c", k=NIC),
                            wT_h[nm].rearrange("(k p) o -> p k o", p=128)
                            [:, :, half * 512:(half + 1) * 512]),
                        waits=P.wW(f"WA{half}") + tuple(extra) +
                              ((("din", spills[nm]),) if gg == 0 else ()),
                        inc="din", inc_by=16)
            P.res(f"WA{half}").setw("din", dd)
            return dd

        RWA = [KT[:, 4096:12288].bitcast(F32), KT[:, 12288:20480].bitcast(F32)]
        _wa_par = [0]

        def wa_direct(nm, half, extra=()):
            par = _wa_par[0] % 2
            _wa_par[0] += 1
            rr = f"rwa{par}"
            dd = P.emit("sync",
                        dma(RWA[par].rearrange("p (k c) -> p k c", k=NIC),
                            w_h[nm].rearrange("(k p) o -> p k o", p=128)
                            [:, :, half * 512:(half + 1) * 512]),
                        waits=P.wW(rr), inc="din", inc_by=16)
            P.res(rr).setw("din", dd)
            vc = P.emit("vector",
                        lambda e, h_=half, par_=par: e.tensor_copy(
                            WA[h_][:, 0:4096], RWA[par_][:, 0:4096]),
                        waits=(("din", dd),) + P.wW(f"WA{half}") + tuple(extra),
                        inc="V")
            P.res(f"WA{half}").setw("V", vc)
            P.res(rr).addr("V", vc)
            sp = P.emit("sync",
                        dma(wT_h[nm].rearrange("(k p) o -> p k o", p=128)
                            [:, :, half * 512:(half + 1) * 512],
                            WA[half].rearrange("p (k c) -> p k c", k=NIC)),
                        waits=(("V", vc),), inc="din", inc_by=16)
            P.res(f"WA{half}").addr("din", sp)
            spills[nm] = sp
            return ("V", vc)

        def wa_get(nm, half, gg, extra=()):
            if gg == 0:
                return wa_direct(nm, half, extra)
            return ("din", wa_load2(nm, half, gg, extra))

        def stage_wq(ng):
            nonlocal wq_d_next
            kw = P.wW(f"wqsK{ng}")
            if ng in (2, 3):
                kw = kw + P.wW("rwg0")
            if ng in (4, 5, 6):
                kw = kw + P.wW("rwg1") + P.wW("stin0") + P.wW("stin1")
            dK = P.emit("sync",
                        dma(KT[:, 4096 * ng: 4096 * ng + 4096]
                            .rearrange("p (k c) -> p k c", k=NIC),
                            wT_h["wqt"].rearrange("(k p) o -> p k o", p=128)
                            [:, :, 0:512]),
                        waits=kw, inc="din", inc_by=16)
            P.res(f"wqsK{ng}").setw("din", dK)
            vw = P.wW(f"wqsV{ng}")
            if ng in (2, 3):
                vw = vw + P.wW("gu2") + P.wW("gu3")
            if ng in (4, 5):
                vw = vw + P.wW("stbf0") + P.wW("stbf1")
            dV = P.emit("sync",
                        dma(VH[:, 4100 * ng: 4100 * ng + 4096]
                            .rearrange("p (k c) -> p k c", k=NIC),
                            wT_h["wqt"].rearrange("(k p) o -> p k o", p=128)
                            [:, :, 512:1024]),
                        waits=vw, inc="din", inc_by=16)
            P.res(f"wqsV{ng}").setw("din", dV)
            wq_d_next = [("din", dK), ("din", dV)]

        def emit_store(ps_):
            gg, xt_, xres_, evac_v = ps_
            so = P.emit("sync",
                        dma(out_h[gg * GT:(gg + 1) * GT, :].rearrange("(t p) c -> p t c", p=128),
                            xt_[:].rearrange("p (t c) -> p t c", t=4)),
                        waits=(("V", evac_v),),
                        inc="dout", inc_by=16)
            P.res(xres_).addr("dout", so)
            dout_of_parity[gg % 2] = P.cnt["dout"]

        pending_store = None
        x_load(0)
        wq_d_next = [wa_get("wqt", 0, 0), wa_get("wqt", 1, 0)]
        ln1_sa_next = emit_ln_pre("ytm", zT, 0, xts[0], "xt0")

        for g in range(NG):
            xt = xts[g % 2]
            xres = f"xt{g % 2}"
            wq_d = wq_d_next
            ln1_sa = ln1_sa_next

            yT_r = yT.rearrange("p (k c) -> p k c", k=NIC)
            emit_ln_post("ytm", zT, yT_r, "yT", ln1_sa)
            if dbg and g == dbg_g:
                allw = (("V", P.cnt["V"]), ("SC", P.cnt["SC"]))
                dbg_dump("d_stats", stats[:], allw)
                dbg_dump("d_yT", yT, allw, ("yT",))

            pool_i = 0

            def qk_proj(w_d, evac, bufs=None, bufres=None):
                nonlocal pool_i
                if bufs is None:
                    bufs = (WA[0], WA[1])
                    bufres = ("WA0", "WA1")
                mm_done = [0, 0]
                for half in range(2):
                    for pl in range(4):
                        pc = half * 4 + pl
                        c0, c1 = pool_cols[pool_i % 4]; pool_i += 1
                        ps = psAll[:, c0:c1]
                        w0, bks = psw(c0, c1)
                        tlast = None
                        for ic in range(NIC):
                            tlast = P.emit(
                                "tensor",
                                lambda e, b_=bufs[half], pl_=pl, ic_=ic, ps_=ps: e.matmul(
                                    ps_,
                                    b_[:, ic_ * 512 + pl_ * 128: ic_ * 512 + (pl_ + 1) * 128],
                                    yT[:, ic_ * 512:(ic_ + 1) * 512],
                                    start=(ic_ == 0), stop=(ic_ == NIC - 1)),
                                waits=((w_d[half],) + P.wR("yT") + w0)
                                      if ic == 0 else (),
                                inc="T" if ic == NIC - 1 else None)
                        for b in bks:
                            P.res(b).setw("T", tlast)
                        P.res("yT").addr("T", tlast)
                        P.res(bufres[half]).addr("T", tlast)
                        sem, n = evac(pc, ps, tlast)
                        for b in bks:
                            P.res(b).addr(sem, n)
                    mm_done[half] = P.cnt["T"]
                return mm_done

            def q_evac(pc, ps, tlast):
                n = P.emit("scalar",
                           lambda e, pc_=pc, ps_=ps: e.activation(
                               QT[:, pc_ * 512:(pc_ + 1) * 512], ps_, AF.Copy, scale=QSCALE),
                           waits=(("T", tlast),) + (P.wW("QT") if pc == 0 else ()),
                           inc="SC")
                P.res("QT").setw("SC", n)
                return ("SC", n)

            if g >= 2:
                q_done = qk_proj(wq_d, q_evac,
                                 bufs=(KT[:, 4096 * g: 4096 * g + 4096],
                                       VH[:, 4100 * g: 4100 * g + 4096]),
                                 bufres=(f"wqsK{g}", f"wqsV{g}"))
                wk_d = [wa_get("wkt", 0, g), wa_get("wkt", 1, g)]
            else:
                q_done = qk_proj(wq_d, q_evac)
                wk_d = [wa_get("wkt", 0, g, (("T", q_done[0]),)),
                        wa_get("wkt", 1, g, (("T", q_done[1]),))]

            def k_evac(pc, ps, tlast):
                extra = ()
                if pc == 0:
                    extra = P.wW("KT")
                    if g in (1, 2, 3):
                        extra = extra + P.wW("rwg0")
                    if g in (4, 5, 6):
                        extra = extra + P.wW("stin0") + P.wW("stin1") + P.wW("rwg1")
                    if g >= 2:
                        extra = extra + P.wW(f"wqsK{g}")
                n = P.emit("vector",
                           lambda e, pc_=pc, ps_=ps, gg=g: e.tensor_copy(
                               KT[:, gg * 4096: (gg + 1) * 4096]
                               .rearrange("p (jj c) -> p jj c", jj=4)
                               [:, :, pc_ * 128:(pc_ + 1) * 128],
                               ps_.rearrange("p (jj c) -> p jj c", jj=4)),
                           waits=(("T", tlast),) + extra,
                           inc="V")
                P.res("KT").setw("V", n)
                return ("V", n)

            if pending_store is not None:
                emit_store(pending_store)
                pending_store = None
            k_done = qk_proj(wk_d, k_evac)
            wv_d = [wa_get("wvt", 0, g, (("T", k_done[0]),)),
                    wa_get("wvt", 1, g, (("T", k_done[1]),))]

            v_done = [0, 0]
            for dh in range(2):
                for ssub in range(4):
                    sidx = 4 * g + ssub
                    c0, c1 = pool_cols[pool_i % 4]; pool_i += 1
                    ps = psAll[:, c0:c1]
                    w0, bks = psw(c0, c1)
                    tlast = None
                    for ic in range(NIC):
                        tlast = P.emit(
                            "tensor",
                            lambda e, dh_=dh, ss_=ssub, ic_=ic, ps_=ps: e.matmul(
                                ps_,
                                yT[:, ic_ * 512 + ss_ * 128: ic_ * 512 + (ss_ + 1) * 128],
                                WA[dh_][:, ic_ * 512:(ic_ + 1) * 512],
                                start=(ic_ == 0), stop=(ic_ == NIC - 1)),
                            waits=((wv_d[dh],) + P.wR("yT") + w0)
                                  if ic == 0 else (),
                            inc="T" if ic == NIC - 1 else None)
                    for b in bks:
                        P.res(b).setw("T", tlast)
                    P.res("yT").addr("T", tlast)
                    P.res(f"WA{dh}").addr("T", tlast)
                    ve = P.emit("vector",
                                lambda e, si=sidx, dh_=dh, ps_=ps: e.tensor_copy(
                                    VH[:, si * VHW + dh_ * 512: si * VHW + (dh_ + 1) * 512],
                                    ps_),
                                waits=(("T", tlast),) +
                                      ((P.wW("VH") +
                                        (P.wW("stbf0") + P.wW("stbf1")
                                         if g in (4, 5) else ()) +
                                        (P.wW("gu2") if g in (1, 2) else ()) +
                                        (P.wW("gu3") if g in (2, 3) else ()) +
                                        (P.wW(f"wqsV{g}") if g >= 2 else ()))
                                       if (dh == 0 and ssub == 0) else ()),
                                inc="V")
                    for b in bks:
                        P.res(b).addr("V", ve)
                    P.res("VH").setw("V", ve)
                v_done[dh] = P.cnt["T"]
            vo = None
            for ssub in range(4):
                sidx = 4 * g + ssub
                vo = P.emit("vector",
                            lambda e, si=sidx: e.memset(
                                VH[:, si * VHW + 1024: si * VHW + 1025], 1.0),
                            inc="V")
            P.res("VH").setw("V", vo)

            wo_d = [wa_get("wot", 0, g, (("T", v_done[0]),)),
                    wa_get("wot", 1, g, (("T", v_done[1]),))]

            if dbg and g == dbg_g:
                allw = (("V", P.cnt["V"]), ("SC", P.cnt["SC"]))
                dbg_dump("d_QT", QT, allw, ("QT",))
                dbg_dump("d_KT", KT[:].rearrange("p (k c) -> p k c", k=NIC)[:, :, 0:512],
                         allw, ("KT",))
                dbg_dump("d_VH", VH[:, 0:VHW], allw, ("VH",))

            qs_ready = P.cnt["SC"]
            kv_ready = P.cnt["V"]
            ct_marks = []
            for qt in range(2):
                NSB = 4 * g + 2 * qt + 2
                NB = NSB // 2
                pv_last = None
                se_of = {}
                fix_of = {}

                def emit_scores(i, qt=qt, NB=NB):
                    par = i % 2
                    sc0 = 512 * par
                    w0, sbks = psw(sc0, sc0 + 512)
                    if qt == 0 and i == 0:
                        w0 = w0 + (("SC", qs_ready), ("V", kv_ready))
                    tlast = None
                    first = True
                    for jj in range(2):
                        j = 2 * i + jj
                        for ic in range(NIC):
                            tlast = P.emit(
                                "tensor",
                                lambda e, j_=j, jj_=jj, ic_=ic, sc0_=sc0, qt_=qt: e.matmul(
                                    psAll[:, sc0_ + jj_ * 256: sc0_ + (jj_ + 1) * 256],
                                    KT[:, j_ * 1024 + ic_ * 128: j_ * 1024 + (ic_ + 1) * 128],
                                    QT[:, ic_ * 512 + qt_ * 256: ic_ * 512 + qt_ * 256 + 256],
                                    start=(ic_ == 0), stop=(ic_ == NIC - 1)),
                                waits=tuple(w0) if first else (),
                                inc="T" if (jj == 1 and ic == NIC - 1) else None)
                            first = False
                    for b in sbks:
                        P.res(b).setw("T", tlast)
                    se = P.emit("scalar",
                                lambda e, sc0_=sc0, par_=par: e.activation(
                                    PT[par_], psAll[:, sc0_:sc0_ + 512], AF.Exp,
                                    bias=consts[:, 0:1]),
                                waits=(("T", tlast),) + P.wW(f"PT{par}"), inc="SC")
                    for b in sbks:
                        P.res(b).addr("SC", se)
                    P.res(f"PT{par}").setw("SC", se)
                    se_of[i] = se
                    vfix = None
                    for jj in range(2):
                        j = 2 * i + jj
                        b_ = j - (4 * g + 2 * qt)
                        if b_ < 0:
                            continue
                        if b_ > 0:
                            P.emit("vector",
                                   lambda e, par_=par, jj_=jj, b__=b_: e.memset(
                                       PT[par_][:, jj_ * 256: jj_ * 256 + b__ * 128], 0.0),
                                   waits=(("SC", se),))
                        vfix = P.emit("vector",
                                      lambda e, par_=par, jj_=jj, b__=b_: e.tensor_tensor(
                                          PT[par_][:, jj_ * 256 + b__ * 128: jj_ * 256 + (b__ + 1) * 128],
                                          PT[par_][:, jj_ * 256 + b__ * 128: jj_ * 256 + (b__ + 1) * 128],
                                          trimask[:], ALU.mult),
                                      waits=(("SC", se), ("din", d_consts)), inc="V")
                        P.res(f"PT{par}").addr("V", vfix)
                    if vfix is not None:
                        fix_of[i] = vfix
                    if dbg and g == dbg_g and qt == 0 and i == 0:
                        dbg_dump("d_PT", PT[0][:, 0:512],
                                 (("SC", se),) + ((("V", vfix),) if vfix else ()),
                                 ("PT0",))

                emit_scores(0)
                for i in range(NB):
                    if i + 1 < NB:
                        emit_scores(i + 1)
                    par = i % 2
                    se = se_of[i]
                    vfix = fix_of.get(i)
                    wpv = [("SC", se)]
                    if vfix is not None:
                        wpv.append(("V", vfix))
                    if i == 0:
                        wc, cbks = psw(1024, 3328)
                        wpv += list(wc) + list(P.wR("VH"))
                    else:
                        cbks = psr(1024, 3328)
                    tpv = None
                    firstpv = True
                    for jj in range(2):
                        j = 2 * i + jj
                        for dc in range(NIC):
                            tpv = P.emit(
                                "tensor",
                                lambda e, j_=j, jj_=jj, dc_=dc, par_=par, i_=i, NB_=NB: e.matmul(
                                    psAll[:, 1024 + dc_ * 256: 1024 + (dc_ + 1) * 256],
                                    VH[:, j_ * VHW + dc_ * 128: j_ * VHW + (dc_ + 1) * 128],
                                    PT[par_][:, jj_ * 256:(jj_ + 1) * 256],
                                    start=(i_ == 0 and jj_ == 0 and dc_ % 2 == 0),
                                    stop=(i_ == NB_ - 1 and jj_ == 1),
                                    skip_group_check=True),
                                waits=tuple(wpv) if firstpv else (),
                                inc=None)
                            firstpv = False
                        tpv = P.emit(
                            "tensor",
                            lambda e, j_=j, jj_=jj, par_=par, i_=i, NB_=NB: e.matmul(
                                psAll[0:1, 3072:3328],
                                VH[:, j_ * VHW + 1024: j_ * VHW + 1025],
                                PT[par_][:, jj_ * 256:(jj_ + 1) * 256],
                                start=(i_ == 0 and jj_ == 0),
                                stop=(i_ == NB_ - 1 and jj_ == 1),
                                skip_group_check=True),
                            inc="T" if jj == 1 else None)
                    for b in cbks:
                        P.res(b).setw("T", tpv)
                    P.res(f"PT{par}").addr("T", tpv)
                    pv_last = tpv
                vr = P.emit("vector",
                            lambda e: e.reciprocal(small[0:1, 0:256], psAll[0:1, 3072:3328]),
                            waits=(("T", pv_last),) + P.wW("rrow"), inc="V")
                P.res("rrow").setw("V", vr)
                for b in psr(3072, 3328):
                    P.res(b).addr("V", vr)
                tb = P.emit("tensor",
                            lambda e: e.matmul(psAll[:, 3328:3584], ones_col[0:1, :],
                                               small[0:1, 0:256], start=True, stop=True),
                            waits=(("V", vr), ("V", v_ones)), inc="T")
                P.res("pb6").setw("T", tb)
                vb = P.emit("vector",
                            lambda e: e.tensor_copy(small[:, 256:512], psAll[:, 3328:3584]),
                            waits=(("T", tb),) + P.wW("bcsb"), inc="V")
                P.res("pb6").addr("V", vb)
                P.res("bcsb").setw("V", vb)
                if dbg and g == dbg_g and qt == 0:
                    dbg_dump("d_sig", small[:, 0:512], (("V", vb),), ("bcsb", "rrow"))
                vm = None
                for dc in range(NIC):
                    vm = P.emit("vector",
                                lambda e, dc_=dc, qt_=qt: e.tensor_tensor(
                                    CT[:, dc_ * 512 + qt_ * 256: dc_ * 512 + qt_ * 256 + 256],
                                    psAll[:, 1024 + dc_ * 256: 1024 + (dc_ + 1) * 256],
                                    small[:, 256:512], ALU.mult),
                                waits=(("V", vb),) +
                                      (P.wW("CT") if (qt == 0 and dc == 0) else ()),
                                inc="V")
                    for b in psr(1024 + dc * 256, 1024 + (dc + 1) * 256):
                        P.res(b).addr("V", vm)
                P.res("bcsb").addr("V", vm)
                P.res("CT").setw("V", vm)
                ct_marks.append(("V", vm))

            P.res("QT").addr("T", P.cnt["T"])
            if dbg and g == dbg_g:
                dbg_dump("d_CT", CT, tuple(ct_marks), ("CT",))

            for ssub in range(4):
                for dh in range(2):
                    c0, c1 = pool_cols[pool_i % 4]; pool_i += 1
                    ps = psAll[:, c0:c1]
                    w0, bks = psw(c0, c1)
                    tlast = None
                    for ic in range(NIC):
                        tlast = P.emit(
                            "tensor",
                            lambda e, dh_=dh, ss_=ssub, ic_=ic, ps_=ps: e.matmul(
                                ps_,
                                CT[:, ic_ * 512 + ss_ * 128: ic_ * 512 + (ss_ + 1) * 128],
                                WA[dh_][:, ic_ * 512:(ic_ + 1) * 512],
                                start=(ic_ == 0), stop=(ic_ == NIC - 1)),
                            waits=(tuple(ct_marks) + (wo_d[dh],) + w0)
                                  if ic == 0 else (),
                            inc="T" if ic == NIC - 1 else None)
                    for b in bks:
                        P.res(b).setw("T", tlast)
                    P.res("CT").addr("T", tlast)
                    P.res(f"WA{dh}").addr("T", tlast)
                    ve = P.emit("vector",
                                lambda e, dh_=dh, ss_=ssub, ps_=ps, xt_=xt: e.scalar_tensor_tensor(
                                    xt_[:, ss_ * 1024 + dh_ * 512: ss_ * 1024 + (dh_ + 1) * 512],
                                    ps_, 1.0,
                                    xt_[:, ss_ * 1024 + dh_ * 512: ss_ * 1024 + (dh_ + 1) * 512],
                                    ALU.mult, ALU.add,
                                    accum_out=stats[:, 160 + ss_ * 16 + 9 + dh_:
                                                    160 + ss_ * 16 + 10 + dh_]),
                                waits=(("T", tlast),), inc="V")
                    for b in bks:
                        P.res(b).addr("V", ve)
                    P.res(xres).setw("V", ve)
                st_ = stats[:, 160 + ssub * 16: 160 + ssub * 16 + 16]
                P.res(f"st160_{ssub}")
                vq = P.emit("vector",
                            lambda e, t_=ssub, stx=st_, xt_=xt: e.scalar_tensor_tensor(
                                xsq[:, 0:1024], xt_[:, t_ * 1024:(t_ + 1) * 1024], 1.0,
                                xt_[:, t_ * 1024:(t_ + 1) * 1024], ALU.mult, ALU.mult,
                                accum_out=stx[:, 1:2]),
                            waits=P.wR(xres) + P.wW("xsq") + P.wW(f"st160_{ssub}"),
                            inc="V")
                P.res("xsq").setw("V", vq)
                P.emit("vector",
                       lambda e, stx=st_: e.tensor_scalar(
                           stx[:, 0:1], stx[:, 9:10], stx[:, 10:11], 1.0,
                           ALU.add, ALU.mult))

            if dbg and g == dbg_g:
                dbg_dump("d_x2", xt[:], (("V", P.cnt["V"]),), (xres,))

            zT_r = zT[:].rearrange("p (k c) -> p k c", k=NIC)
            ln2_sa = ln_chain_apply("ztm", HT, 160, xt)
            emit_ln_post("ztm", HT, zT_r, "zT", ln2_sa)
            if dbg and g == dbg_g:
                dbg_dump("d_zT", zT[:], (("V", P.cnt["V"]), ("SC", P.cnt["SC"])), ("zT",))

            gu_d = []

            if g == 0:
                SLW, HPS, NSL = 768, 6, 4
                gu_bufs = [(WF[0], "WF0", WF[1], "WF1"),
                           (VH[:, 4100:10244], "gu2", VH[:, 10244:16388], "gu3")]

                gu_pending = {}

                def gu_flush_spill(br):
                    nm, buf, sl_, vc = gu_pending.pop(br)
                    sp = P.emit("sync",
                                dma(wT_h[nm].rearrange("(k p) hh -> p k hh", p=128)
                                    [:, :, sl_ * SLW:(sl_ + 1) * SLW],
                                    buf[:, 0:6144].rearrange("p (k c) -> p k c", k=NIC)),
                                waits=(("V", vc),), inc="din", inc_by=16)
                    P.res(br).addr("din", sp)
                    spills[nm] = sp

                def gu_slab_load(sl):
                    gb, gr, ub, ur = gu_bufs[sl % 2]
                    pair = []
                    for nm, buf, br, rw in (("wgt", gb, gr, 0), ("wut", ub, ur, 1)):
                        dd = P.emit("sync",
                                    dma(RWGU[rw].rearrange("p (k c) -> p k c", k=NIC),
                                        w_h[nm].rearrange("(k p) hh -> p k hh", p=128)
                                        [:, :, sl * SLW:(sl + 1) * SLW]),
                                    waits=P.wW(f"rwg{rw}") +
                                          ((P.wW("stin0") + P.wW("stin1"))
                                           if rw == 1 else ()),
                                    inc="din", inc_by=16)
                        P.res(f"rwg{rw}").setw("din", dd)
                        if br in gu_pending:
                            gu_flush_spill(br)
                        vc = P.emit("vector",
                                    lambda e, b_=buf, rw_=rw: e.tensor_copy(
                                        b_[:, 0:6144], RWGU[rw_][:, 0:6144]),
                                    waits=(("din", dd),) + P.wW(br), inc="V")
                        P.res(br).setw("V", vc)
                        P.res(f"rwg{rw}").addr("V", vc)
                        gu_pending[br] = (nm, buf, sl, vc)
                        pair.append(("V", vc))
                    gu_d.append(tuple(pair))

                def gu_lhs(sl, half, ic, hcl):
                    gb, gr, ub, ur = gu_bufs[sl % 2]
                    buf = gb if half == 0 else ub
                    return buf[:, ic * SLW + hcl * 128: ic * SLW + (hcl + 1) * 128]

                def gu_mark_readers(sl, tu):
                    gb, gr, ub, ur = gu_bufs[sl % 2]
                    P.res(gr).addr("T", tu)
                    P.res(ur).addr("T", tu)
            else:
                SLW, HPS, NSL = 384, 3, 8

                def gu_slab_load(sl):
                    p_ = sl % 2
                    wgd = P.emit("sync",
                                 dma(WF[p_][:, 0:3072].rearrange("p (k c) -> p k c", k=NIC),
                                     wT_h["wgt"].rearrange("(k p) hh -> p k hh", p=128)
                                     [:, :, sl * 384:(sl + 1) * 384]),
                                 waits=P.wW(f"WF{p_}"),
                                 inc="din", inc_by=16)
                    wud = P.emit("sync",
                                 dma(WF[p_][:, 3072:6144].rearrange("p (k c) -> p k c", k=NIC),
                                     wT_h["wut"].rearrange("(k p) hh -> p k hh", p=128)
                                     [:, :, sl * 384:(sl + 1) * 384]),
                                 inc="din", inc_by=16)
                    P.res(f"WF{p_}").setw("din", wud)
                    gu_d.append((("din", wgd), ("din", wud)))

                def gu_lhs(sl, half, ic, hcl):
                    p_ = sl % 2
                    base = half * 3072 + ic * SLW + hcl * 128
                    return WF[p_][:, base: base + 128]

                def gu_mark_readers(sl, tu):
                    P.res(f"WF{sl % 2}").addr("T", tu)

            gu_slab_load(0)
            gu_slab_load(1)

            if g + 1 < NG:
                x_load(g + 1)
            if 1 <= g < NG - 1:
                stage_wq(g + 1)

            GUC = [(2048, 2560), (2560, 3072)]
            h_done_v = 0
            for hc in range(NHC):
                sl, hcl = hc // HPS, hc % HPS
                wgw, wuw = gu_d[sl]
                gw, gbks = psw(*GUC[0])
                tg = None
                for ic in range(NIC):
                    tg = P.emit("tensor",
                                lambda e, ic_=ic, a_=gu_lhs(sl, 0, ic, hcl): e.matmul(
                                    psAll[:, GUC[0][0]:GUC[0][1]],
                                    a_,
                                    zT[:, ic_ * 512:(ic_ + 1) * 512],
                                    start=(ic_ == 0), stop=(ic_ == NIC - 1)),
                                waits=((wgw,) + P.wR("zT") + gw)
                                      if ic == 0 else (),
                                inc="T" if ic == NIC - 1 else None)
                for b in gbks:
                    P.res(b).setw("T", tg)
                P.res("zT").addr("T", tg)
                uw, ubks = psw(*GUC[1])
                tu = None
                for ic in range(NIC):
                    tu = P.emit("tensor",
                                lambda e, ic_=ic, a_=gu_lhs(sl, 1, ic, hcl): e.matmul(
                                    psAll[:, GUC[1][0]:GUC[1][1]],
                                    a_,
                                    zT[:, ic_ * 512:(ic_ + 1) * 512],
                                    start=(ic_ == 0), stop=(ic_ == NIC - 1)),
                                waits=((wuw,) + uw) if ic == 0 else (),
                                inc="T" if ic == NIC - 1 else None)
                for b in ubks:
                    P.res(b).setw("T", tu)
                if hcl == HPS - 1:
                    gu_mark_readers(sl, tu)
                    if sl + 2 < NSL:
                        gu_slab_load(sl + 2)
                ssil = P.emit("scalar",
                              lambda e: e.activation(sg[:], psAll[:, GUC[0][0]:GUC[0][1]],
                                                     AF.Silu, scale=DEQ),
                              waits=(("T", tg),) + P.wW("sg"), inc="SC")
                for b in gbks:
                    P.res(b).addr("SC", ssil)
                P.res("sg").setw("SC", ssil)
                vh_ = P.emit("vector",
                             lambda e, hc_=hc: e.tensor_tensor(
                                 HT[:, hc_ * 512:(hc_ + 1) * 512], sg[:],
                                 psAll[:, GUC[1][0]:GUC[1][1]], ALU.mult),
                             waits=(("SC", ssil), ("T", tu)) +
                                   (P.wW("HT") if hc == 0 else ()),
                             inc="V")
                for b in ubks:
                    P.res(b).addr("V", vh_)
                P.res("sg").addr("V", vh_)
                P.res("HT").setw("V", vh_)
                h_done_v = vh_

            if g == 0:
                for br_ in list(gu_pending):
                    gu_flush_spill(br_)

            if dbg and g == dbg_g:
                dbg_dump("d_HT", HT[:, 0:4096], (("V", P.cnt["V"]),), ("HT",))

            if g + 1 < NG:
                ln1_sa_next = emit_ln_pre("ytm", zT, 0, xts[(g + 1) % 2],
                                          f"xt{(g + 1) % 2}")

            for dh in range(2):
                wd_d = []

                if g == 0:
                    def wd_slab_load(sl, dh=None):
                        par = sl % 2
                        rr = "rwdA" if par == 0 else "rwdB"
                        dd = P.emit("sync",
                                    dma(RWD[par].rearrange("p (k c) -> p k c", k=6),
                                        w_h["wdt"].rearrange("(k p) dd -> p k dd", p=128)
                                        [:, 6 * sl:6 * sl + 6, dh * 512:(dh + 1) * 512]),
                                    waits=P.wW(rr), inc="din", inc_by=16)
                        P.res(rr).setw("din", dd)
                        vc = P.emit("vector",
                                    lambda e, par_=par: e.tensor_copy(
                                        WF[par_][:, 0:3072], RWD[par_][:, 0:3072]),
                                    waits=(("din", dd),) + P.wW(f"WF{par}"), inc="V")
                        P.res(f"WF{par}").setw("V", vc)
                        P.res(rr).addr("V", vc)
                        sp = P.emit("sync",
                                    dma(wT_h["wdt"].rearrange("(k p) dd -> p k dd", p=128)
                                        [:, 6 * sl:6 * sl + 6, dh * 512:(dh + 1) * 512],
                                        WF[par][:, 0:3072].rearrange("p (k c) -> p k c", k=6)),
                                    waits=(("V", vc),), inc="din", inc_by=16)
                        P.res(f"WF{par}").addr("din", sp)
                        spills["wdt"] = sp
                        wd_d.append(("V", vc))
                else:
                    def wd_slab_load(sl, dh=None):
                        p_ = sl % 2
                        dd = P.emit("sync",
                                    dma(WF[p_][:, 0:3072].rearrange("p (k c) -> p k c", k=6),
                                        wT_h["wdt"].rearrange("(k p) dd -> p k dd", p=128)
                                        [:, 6 * sl:6 * sl + 6, dh * 512:(dh + 1) * 512]),
                                    waits=P.wW(f"WF{p_}"),
                                    inc="din", inc_by=16)
                        P.res(f"WF{p_}").setw("din", dd)
                        wd_d.append(("din", dd))

                wd_slab_load(0, dh)
                wd_slab_load(1, dh)
                bank_w0 = {}
                for ssub in range(4):
                    c0, c1 = pool_cols[ssub]
                    bank_w0[ssub] = psw(c0, c1)[0]
                for hc in range(NHC):
                    sl, hcl = hc // 6, hc % 6
                    p_ = sl % 2
                    for ssub in range(4):
                        c0, c1 = pool_cols[ssub]
                        last_of_slab = (hcl == 5 and ssub == 3)
                        tl = P.emit(
                            "tensor",
                            lambda e, hc_=hc, hcl_=hcl, ss_=ssub, p__=p_, c0_=c0, c1_=c1: e.matmul(
                                psAll[:, c0_:c1_],
                                HT[:, hc_ * 512 + ss_ * 128: hc_ * 512 + (ss_ + 1) * 128],
                                WF[p__][:, hcl_ * 512:(hcl_ + 1) * 512],
                                start=(hc_ == 0), stop=(hc_ == NHC - 1)),
                            waits=((wd_d[sl], ("V", h_done_v)) + bank_w0[ssub])
                                  if hc == 0 else
                                  ((wd_d[sl],) if (hcl == 0 and ssub == 0) else ()),
                            inc="T" if last_of_slab else None)
                        if last_of_slab:
                            P.res(f"WF{p_}").addr("T", tl)
                            P.res("HT").addr("T", tl)
                            if sl + 2 < 4:
                                wd_slab_load(sl + 2, dh)
                t_down = P.cnt["T"]
                for ssub in range(4):
                    c0, c1 = pool_cols[ssub]
                    for b in psr(c0, c1):
                        P.res(b).setw("T", t_down)
                    ve = P.emit("vector",
                                lambda e, dh_=dh, ss_=ssub, c0_=c0, c1_=c1, xt_=xt: e.scalar_tensor_tensor(
                                    xt_[:, ss_ * 1024 + dh_ * 512: ss_ * 1024 + (dh_ + 1) * 512],
                                    psAll[:, c0_:c1_], DEQ2,
                                    xt_[:, ss_ * 1024 + dh_ * 512: ss_ * 1024 + (dh_ + 1) * 512],
                                    ALU.mult, ALU.add),
                                waits=(("T", t_down),), inc="V")
                    for b in psr(c0, c1):
                        P.res(b).addr("V", ve)
                    P.res(xres).setw("V", ve)
            last_ffn_evac_v = P.cnt["V"]

            if g + 1 == 1:
                wq_d_next = [wa_get("wqt", 0, g + 1), wa_get("wqt", 1, g + 1)]

            pending_store = (g, xt, xres, last_ffn_evac_v)
            if g == NG - 1:
                emit_store(pending_store)
                pending_store = None

        final_dout = P.cnt["dout"]
        P.emit("sync", lambda e: e.nop(), waits=(("dout", final_dout),))
        P.cut()

        block2 = nc.Block()
        with block2:
            for ename in P.ENGINES:
                lo, hi = P.section[ename][0], P.section[ename][1]
                ops = P.streams[ename][lo:hi]
                if not ops:
                    continue

                def body(eng, ops=ops):
                    for waits, fn, inc, inc_by in ops:
                        for s_, v_ in waits:
                            eng.wait_ge(semh[s_], v_)
                        inst = fn(eng)
                        if inc is not None:
                            inst.then_inc(semh[inc], inc_by)

                getattr(block2, ename)(body)

    return nc


_NC_CACHE = None


def _consts():
    ident = np.eye(128, dtype=ml_dtypes.bfloat16)
    tri = np.triu(np.ones((128, 128), dtype=np.float32)).astype(ml_dtypes.bfloat16)
    consts = np.zeros((128, 4), dtype=np.float32)
    consts[:, 0] = ESHIFT
    consts[:, 1] = EPS
    return ident, tri, consts


def kernel(L0_attn_in_features, L0_attn_w_q, L0_attn_w_k, L0_attn_w_v, L0_attn_w_o,
           L0_ffn_w_gate, L0_ffn_w_up, L0_ffn_w_down):
    global _NC_CACHE
    x = np.ascontiguousarray(np.asarray(L0_attn_in_features, dtype=np.float32))
    wq = np.ascontiguousarray(np.asarray(L0_attn_w_q, dtype=np.float32).T)
    wk = np.ascontiguousarray(np.asarray(L0_attn_w_k, dtype=np.float32).T)
    wv = np.ascontiguousarray(np.asarray(L0_attn_w_v, dtype=np.float32).T)
    wo = np.ascontiguousarray(np.asarray(L0_attn_w_o, dtype=np.float32).T)
    wg = np.ascontiguousarray(np.asarray(L0_ffn_w_gate, dtype=np.int32).T)
    wu = np.ascontiguousarray(np.asarray(L0_ffn_w_up, dtype=np.int32).T)
    wd = np.ascontiguousarray(np.asarray(L0_ffn_w_down, dtype=np.int32).T)
    ident, tri, consts = _consts()

    if _NC_CACHE is None:
        _NC_CACHE = build_nc()
    nc = _NC_CACHE

    in_maps = []
    for c in range(8):
        in_maps.append({
            "x": x[c], "wqt": wq, "wkt": wk, "wvt": wv, "wot": wo,
            "wgt": wg, "wut": wu, "wdt": wd,
            "ident": ident, "trimask": tri, "consts": consts,
        })
    res = run_bass_kernel_spmd(nc, in_maps, core_ids=list(range(8)))
    out = np.stack([res.results[c]["out"] for c in range(8)], axis=0)
    return out


def kernel_dbg(inputs, dbg_g=0):
    x = np.ascontiguousarray(np.asarray(inputs["L0_attn_in_features"], dtype=np.float32))
    ident, tri, consts = _consts()
    nc = build_nc(dbg=True, dbg_g=dbg_g)
    im = {
        "x": x[0],
        "wqt": np.ascontiguousarray(np.asarray(inputs["L0_attn_w_q"], dtype=np.float32).T),
        "wkt": np.ascontiguousarray(np.asarray(inputs["L0_attn_w_k"], dtype=np.float32).T),
        "wvt": np.ascontiguousarray(np.asarray(inputs["L0_attn_w_v"], dtype=np.float32).T),
        "wot": np.ascontiguousarray(np.asarray(inputs["L0_attn_w_o"], dtype=np.float32).T),
        "wgt": np.ascontiguousarray(np.asarray(inputs["L0_ffn_w_gate"], dtype=np.int32).T),
        "wut": np.ascontiguousarray(np.asarray(inputs["L0_ffn_w_up"], dtype=np.int32).T),
        "wdt": np.ascontiguousarray(np.asarray(inputs["L0_ffn_w_down"], dtype=np.int32).T),
        "ident": ident, "trimask": tri, "consts": consts,
    }
    res = run_bass_kernel_spmd(nc, [im], core_ids=[0])
    return res.results[0]
